# version 13
# speedup vs baseline: 1.5189x; 1.0073x over previous
import os as _os
import numpy as np
import ml_dtypes
from contextlib import ExitStack

_DBG = _os.environ.get("KDBG", "")

import concourse.bass as bass
import concourse.mybir as mybir
from concourse.bass_utils import run_bass_kernel_spmd

F32 = mybir.dt.float32
BF16 = mybir.dt.bfloat16
AX = mybir.AxisListType.X
ALU = mybir.AluOpType
ACT = mybir.ActivationFunctionType
BF = ml_dtypes.bfloat16

S = 2048
D = 1024
NT = S // 128
DC = D // 128
FF = 4096
NH = 16
HD = 64
NEG = -1.0e30
ARENA_N = 53208

ENGINES = ("tensor", "vector", "scalar", "sync", "gpsimd")

_CONST = {}


def host_consts():
    if _CONST:
        return _CONST
    pos = np.arange(S, dtype=np.float64)
    i = np.arange(D // 2, dtype=np.float64)
    theta0 = np.power(10000.0, -2.0 * (i - 1.0) / D)
    ang0 = pos[:, None] * theta0[None, :]
    freqs1 = 1.0 / np.power(10000.0, i / (D // 2))
    ang1 = pos[:, None] * freqs1[None, :]
    c = {}
    c["c0k"] = np.cos(ang0).astype(np.float32); c["s0k"] = np.sin(ang0).astype(np.float32)
    c["c0q"] = (np.cos(ang0) / 32.0).astype(np.float32); c["s0q"] = (np.sin(ang0) / 32.0).astype(np.float32)
    c["c1k"] = np.cos(ang1).astype(np.float32); c["s1k"] = np.sin(ang1).astype(np.float32)
    c["c1q"] = (np.cos(ang1) / 8.0).astype(np.float32); c["s1q"] = (np.sin(ang1) / 8.0).astype(np.float32)
    c["identb"] = np.eye(128, dtype=BF)
    c["identf"] = np.eye(128, dtype=np.float32)
    m = np.zeros((128, 128), np.float32)
    r = np.arange(128)
    m[r[:, None] < r[None, :]] = NEG
    c["mask"] = m
    c["maskT"] = np.ascontiguousarray(m.T)
    c["onesb"] = np.ones((128, 256), BF)
    slopes = np.array([2.0 ** (-0.5 * (h + 1)) for h in range(NH)], np.float64)
    kpos_hi = pos.astype(BF).astype(np.float64)
    kpos_lo = (pos - kpos_hi).astype(BF).astype(np.float64)
    c["kaug"] = np.ascontiguousarray(
        np.stack([np.ones(S, np.float64),
                  kpos_hi, kpos_lo, kpos_hi, kpos_lo]).astype(BF))
    s_hi = slopes.astype(BF).astype(np.float64)
    s_lo = (slopes - s_hi).astype(BF)
    qaug = np.zeros((5, NH * 128), BF)
    qaug[0, :] = BF(1.0)
    for h in range(NH):
        qaug[1, h * 128:(h + 1) * 128] = BF(s_hi[h])
        qaug[2, h * 128:(h + 1) * 128] = BF(s_hi[h])
        qaug[3, h * 128:(h + 1) * 128] = s_lo[h]
        qaug[4, h * 128:(h + 1) * 128] = s_lo[h]
    c["qaug"] = qaug
    qs = np.zeros((128, NT * NH), np.float32)
    for qt in range(NT):
        p = qt * 128 + np.arange(128, dtype=np.float64)
        for h in range(NH):
            qs[:, qt * NH + h] = (13.0 - slopes[h] * p).astype(np.float32)
    c["qslope"] = qs
    c["onesf"] = np.ones((1, 128), np.float32)
    _CONST.update(c)
    return c


DMA_POOL = {"sync": 14, "gpsimd": 8, "scalar": 2, "tensor": 1, "vector": 1}


class Sch:
    def __init__(self):
        self.items = {e: [] for e in ENGINES}
        self.ninc = {e: 0 for e in ENGINES}
        self.rr = {e: 0 for e in ENGINES}
        self.slot_cnt = {e: [0] * DMA_POOL[e] for e in ENGINES}
        self.slot_cons = {e: [None] * DMA_POOL[e] for e in ENGINES}

    def op(self, eng, fn, deps=(), inc=True):
        self.items[eng].append(("op", fn, deps, None, inc))
        if inc:
            self.ninc[eng] += 1
            return (eng, self.ninc[eng])
        return None

    def dma(self, eng, fn, deps=()):
        slot = self.rr[eng] % DMA_POOL[eng]
        self.rr[eng] += 1
        k = self.slot_cnt[eng][slot]
        deps = list(deps)
        if k > 0:
            c = self.slot_cons[eng][slot]
            deps.append(c if c is not None else (("dq", eng, slot), k))
        self.slot_cnt[eng][slot] = k + 1
        self.slot_cons[eng][slot] = None
        self.items[eng].append(("dma", fn, _prune(deps), slot, True))
        return (("dq", eng, slot), k + 1)

    def note_consumer(self, dep, tok):
        kind, n = dep
        if isinstance(kind, tuple) and kind[0] == "dq" and tok is not None:
            _, eng, slot = kind
            if n == self.slot_cnt[eng][slot] and self.slot_cons[eng][slot] is None:
                self.slot_cons[eng][slot] = tok

    def dma_all_tokens(self, eng):
        return [(("dq", eng, s), c) for s, c in enumerate(self.slot_cnt[eng]) if c > 0]


class Buf:
    __slots__ = ("w", "r", "fresh")
    def __init__(self):
        self.w = []
        self.r = []
        self.fresh = True


def _prune(deps):
    best = {}
    for k, n in deps:
        if n > best.get(k, -1):
            best[k] = n
    return tuple(best.items())


class B:
    def __init__(self, sch):
        self.sch = sch
        self.guard = ()

    def set_guard(self):
        g = [(e, self.sch.ninc[e]) for e in ENGINES if self.sch.ninc[e] > 0]
        for e in ENGINES:
            g += self.sch.dma_all_tokens(e)
        self.guard = tuple(g)

    def _deps(self, reads, writes, extra):
        deps = []
        for b in reads:
            deps += b.w
        for b in writes:
            deps += b.w + b.r
            if b.fresh:
                deps += list(self.guard)
        return _prune(deps + [d for d in extra if d is not None])

    def op(self, eng, fn, reads=(), writes=(), extra=(), dma=False):
        deps = self._deps(reads, writes, extra)
        tok = self.sch.dma(eng, fn, deps) if dma else self.sch.op(eng, fn, deps, True)
        if not dma:
            for d in deps:
                self.sch.note_consumer(d, tok)
        for b in reads:
            b.r.append(tok)
        for b in writes:
            b.w = [tok]
            b.r = []
            b.fresh = False
        return tok

    def group(self, eng, fns, reads=(), writes=(), extra=()):
        deps = self._deps(reads, writes, extra)
        tok = None
        for i, fn in enumerate(fns):
            t = self.sch.op(eng, fn, deps if i == 0 else (), i == len(fns) - 1)
            if t is not None:
                tok = t
        for d in deps:
            self.sch.note_consumer(d, tok)
        for b in reads:
            b.r.append(tok)
        for b in writes:
            b.w = [tok]
            b.r = []
            b.fresh = False
        return tok


class Arena:
    def __init__(self, ap, n):
        self.ap = ap
        self.n = n
        self.off = 0
        self.peak = 0

    def mark(self):
        return self.off

    def release(self, m):
        self.off = m

    def f32(self, n, parts=128):
        n = (n + 15) & ~15
        assert self.off + n <= self.n, f"arena overflow: {self.off}+{n}>{self.n}"
        ap = self.ap[0:parts, self.off:self.off + n]
        self.off += n
        self.peak = max(self.peak, self.off)
        return ap

    def b16(self, n, parts=128):
        nf = ((n + 1) // 2 + 15) & ~15
        base = self.f32(nf, 128)
        return base.bitcast(BF16)[0:parts, 0:n]


def build_program():
    nc = bass.Bass("TRN2", target_bir_lowering=False)

    def din(name, shape, dt=F32):
        return nc.dram_tensor(name, list(shape), dt, kind="ExternalInput").ap()

    x_in = din("x_in", (S, D))
    w0q = din("w0q", (D, D)); w0k = din("w0k", (D, D)); w0v = din("w0v", (D, D))
    W1h = din("W1", (FF, D)); b1l = din("b1l", (128, FF // 128))
    W2h = din("W2", (D, FF)); b2r = din("b2r", (128, D))
    w1q = din("w1q", (D, D)); w1k = din("w1k", (D, D)); w1v = din("w1v", (D, D))
    w1o = din("w1o", (D, D)); outbr = din("outbr", (128, D))
    wf1 = din("wf1", (D, D)); gamr = din("gamr", (128, D)); betr = din("betr", (128, D))
    tabs = {k: din("t_" + k, (S, D // 2)) for k in
            ("c0k", "s0k", "c0q", "s0q", "c1k", "s1k", "c1q", "s1q")}
    identb_h = din("identb", (128, 128), BF16)
    identf_h = din("identf", (128, 128))
    mask_h = din("maskt", (128, 128))
    maskT_h = din("masktt", (128, 128))
    onesb_h = din("onesb", (128, 256), BF16)
    kaug_h = din("kaugt", (5, S), BF16)
    qaug_h = din("qaugt", (5, NH * 128), BF16)
    qsl_h = din("qslt", (128, NT * NH))
    ones_h = din("onesf", (1, 128))
    epsv_h = din("epsv", (128, 2))

    out_h = nc.dram_tensor("out", [S, D], F32, kind="ExternalOutput").ap()
    xsp = nc.dram_tensor("xspill", [S, D], F32, kind="Internal").ap()
    qsp = nc.dram_tensor("qspill", [NT, 69, NH * 128], BF16, kind="Internal").ap()
    ccin = nc.dram_tensor("ccin", [1, 1], F32, kind="Internal").ap()
    ccout = nc.dram_tensor("ccout", [1, 1], F32, kind="Internal").ap()
    x3 = x_in.rearrange("(n p) d -> n p d", p=128)
    xsp3 = xsp.rearrange("(n p) d -> n p d", p=128)
    out3 = out_h.rearrange("(n p) d -> n p d", p=128)
    t3 = {k: v.rearrange("(n p) d -> n p d", p=128) for k, v in tabs.items()}

    sch = Sch()
    bb = B(sch)
    st = ExitStack()

    arena_t = st.enter_context(nc.sbuf_tensor("arena", [128, ARENA_N], F32))
    ar = Arena(arena_t.ap(), ARENA_N)

    psA_t = st.enter_context(nc.psum_tensor("psA", [128, 1024], F32))
    psB_t = st.enter_context(nc.psum_tensor("psB", [128, 1024], F32))
    psb_t = st.enter_context(nc.psum_tensor("psbig", [128, 1024], F32))
    ptr_t = st.enter_context(nc.psum_tensor("ptrb", [128, 512], F32))
    psm_t = st.enter_context(nc.psum_tensor("psmb", [128, 512], F32))
    psA = psA_t.ap(); psB = psB_t.ap(); psb = psb_t.ap()
    ptrF = ptr_t.ap(); psmF = psm_t.ap()
    ptrB = ptrF.bitcast(BF16); psmB = psmF.bitcast(BF16)
    bA, bB, bG, bT, bM = Buf(), Buf(), Buf(), Buf(), Buf()

    csem = {e: st.enter_context(nc.semaphore("s_" + e)) for e in ENGINES}
    dsem = {}
    for e in ENGINES:
        for sl_ in range(DMA_POOL[e]):
            if e in ("sync", "gpsimd", "scalar"):
                dsem[(e, sl_)] = st.enter_context(
                    nc.semaphore("d_%s_%d" % (e, sl_)))

    identb = ar.b16(128)
    identf = ar.f32(128)
    mask_s = ar.f32(128)
    maskT_s = ar.f32(128)
    qsl_s = ar.f32(NT * NH)
    ones_s = ar.f32(16)
    ones_s = ar.f32(128)[0:1, :]
    epsv_s = ar.f32(16)[:, 0:2]
    (bid, bidf, bmsk, bmskT, bqsl, bone, beps) = (Buf() for _ in range(7))

    class Slots:
        def __init__(self):
            a = ar.f32(40)
            self.st6 = a[:, 0:12]
            self.mv = a[:, 12:14]
            self.tmp = a[:, 14:15]
            self.sg = a[:, 15:16]
            self.ngm = a[:, 16:17]
            self.lc = a[:, 17:18]
            self.l2 = a[:, 18:19]
            self.lp = a[:, 20:24]
            self.beta = a[:, 24:40]
            (self.bst6, self.bmv, self.btmp, self.bsg, self.bngm, self.blc,
             self.bl2, self.blp, self.bbeta) = (Buf() for _ in range(9))

    SL = None
    v_mk = ar.f32(16)
    kn_run = ar.f32(16)
    kn_t = ar.f32(16)
    qn_t = ar.f32(16)
    mkb_s = ar.f32(16)
    pv_s = ar.f32(16)
    bmk, bknr, bknt, bqnt, bmkb, bpv = (Buf() for _ in range(6))

    def _realize(block):
        def mk_body(en):
            def body(e):
                waited = {}
                for kind, fn, deps, slot, inc in sch.items[en]:
                    for dk, dn in deps:
                        if isinstance(dk, tuple):
                            sem, tgt = dsem[(dk[1], dk[2])], 16 * dn
                        else:
                            sem, tgt = csem[dk], dn
                        if waited.get(dk, -1) >= tgt:
                            continue
                        e.wait_ge(sem, tgt)
                        waited[dk] = tgt
                    inst = fn(e)
                    if kind == "dma":
                        inst.then_inc(dsem[(en, slot)], 16)
                    elif inc:
                        inst.then_inc(csem[en], 1)
            return body
        block.tensor(mk_body("tensor"))
        block.vector(mk_body("vector"))
        block.scalar(mk_body("scalar"))
        block.sync(mk_body("sync"))
        block.gpsimd(mk_body("gpsimd"))

    def ld(eng, dst, src, buf, extra=()):
        return bb.op(eng, lambda e, d=dst, s=src: e.dma_start(out=d, in_=s),
                     writes=(buf,), extra=extra, dma=True)

    ld("sync", identb, identb_h[:], bid)
    ld("sync", identf, identf_h[:], bidf)
    ld("sync", mask_s, mask_h[:], bmsk)
    ld("sync", maskT_s, maskT_h[:], bmskT)
    ld("sync", qsl_s, qsl_h[:], bqsl)
    ld("sync", ones_s, ones_h[:], bone)
    ld("sync", epsv_s, epsv_h[:], beps)

    def v_op(fn, reads=(), writes=(), extra=()):
        return bb.op("vector", fn, reads, writes, extra)

    def s_op(fn, reads=(), writes=(), extra=()):
        return bb.op("scalar", fn, reads, writes, extra)

    def t_op(fn, reads=(), writes=(), extra=()):
        return bb.op("tensor", fn, reads, writes, extra)

    def rsqrt_chain(sl, dst, dstbuf, var_ap, varbuf, scale, eps_col):
        s_op(lambda e: e.activation(out=sl.tmp, in_=var_ap, func=ACT.Ln,
                                    scale=scale, bias=epsv_s[:, eps_col:eps_col + 1]),
             reads=(varbuf, beps), writes=(sl.btmp,))
        s_op(lambda e: e.activation(out=dst, in_=sl.tmp, func=ACT.Exp, scale=-0.5),
             reads=(sl.btmp,), writes=(dstbuf,))

    def stats(sl, src_ap, srcbuf):
        v_op(lambda e: e.bn_stats(sl.st6[:, 0:6], src_ap[:, 0:512]),
             reads=(srcbuf,), writes=(sl.bst6,))
        v_op(lambda e: e.bn_stats(sl.st6[:, 6:12], src_ap[:, 512:1024]),
             reads=(srcbuf,), writes=(sl.bst6,))
        v_op(lambda e: e.bn_aggr(sl.mv, sl.st6), reads=(sl.bst6,), writes=(sl.bmv,))

    def ln_apply(sl, dst_ap, dstbuf, src_ap, srcbuf, var_scale=1.0, eps_col=0):
        rsqrt_chain(sl, sl.sg, sl.bsg, sl.mv[:, 1:2], sl.bmv, var_scale, eps_col)
        v_op(lambda e: e.tensor_scalar(sl.ngm, sl.mv[:, 0:1], sl.sg, -1.0,
                                       op0=ALU.mult, op1=ALU.mult),
             reads=(sl.bmv, sl.bsg), writes=(sl.bngm,))
        s_op(lambda e: e.activation(out=dst_ap, in_=src_ap, func=ACT.Identity,
                                    scale=sl.sg, bias=sl.ngm),
             reads=(srcbuf, sl.bsg, sl.bngm), writes=(dstbuf,))

    SL = [Slots(), Slots()]

    class LnTab:
        def __init__(self):
            a = ar.f32(32)
            self.sg = a[:, 0:16]
            self.ngm = a[:, 16:32]
            self.b = Buf()
    _ltA, _ltB = LnTab(), LnTab()
    LT = {"l0": _ltA, "l1": _ltA, "f0": _ltB, "f1": _ltB}

    def ln_precompute(key, t, x_ap, xbuf, sl, var_scale=1.0, eps_col=0):
        lt = LT[key]
        stats(sl, x_ap, xbuf)
        rsqrt_chain(sl, lt.sg[:, t:t + 1], lt.b, sl.mv[:, 1:2], sl.bmv,
                    var_scale, eps_col)
        v_op(lambda e, lt=lt, t=t, sl=sl: e.tensor_scalar(
            lt.ngm[:, t:t + 1], sl.mv[:, 0:1], lt.sg[:, t:t + 1], -1.0,
            op0=ALU.mult, op1=ALU.mult),
            reads=(sl.bmv, lt.b), writes=(lt.b,))

    def ln_apply_pre(key, t, dst_ap, dstbuf, x_ap, xbuf):
        lt = LT[key]
        return s_op(lambda e, lt=lt, t=t: e.activation(
            out=dst_ap, in_=x_ap, func=ACT.Identity,
            scale=lt.sg[:, t:t + 1], bias=lt.ngm[:, t:t + 1]),
            reads=(xbuf, lt.b), writes=(dstbuf,))

    m_x = ar.mark()
    x_t = [ar.f32(1024) for _ in range(NT)]
    x_b = [Buf() for _ in range(NT)]
    for t in range(NT):
        ld("sync", x_t[t], x3[t], x_b[t])

    m_attn = ar.mark()

    def attention(layer, get_x, put_x, lnkey, lnout):
        L1 = layer == 1
        m0 = ar.mark()
        if L1:
            kaug = [ar.b16(S, parts=68) for _ in range(NH)]
            kaug_b = [Buf() for _ in range(NH)]
        else:
            kfm = ar.b16(DC * S)
            bkfm = Buf()
        vtm = ar.b16(NT * D)
        bvtm = Buf()
        wqT = ar.b16(DC * D)
        bwqT = Buf()
        if L1:
            woT = ar.b16(DC * D)
            bwoT = Buf()
        xn16 = [ar.b16(D), ar.b16(D)]
        bxn = [Buf(), Buf()]
        xnfm = [ar.b16(DC * 128), ar.b16(DC * 128)]
        bxnfm = [Buf(), Buf()]
        rsc0 = ar.f32(1024)
        rsc = [rsc0, rsc0]
        brsc0 = Buf()
        brsc = [brsc0, brsc0]
        rot16 = [ar.b16(D), ar.b16(D)]
        brot = [Buf(), Buf()]
        tabc = [ar.f32(512), ar.f32(512)]
        tabs_ = [ar.f32(512), ar.f32(512)]
        btc = [Buf(), Buf()]; bts = [Buf(), Buf()]
        m_kv = ar.mark()
        wkT = ar.b16(DC * D)
        bwkT = Buf()
        wvT = ar.b16(DC * D)
        bwvT = Buf()
        m_stage = ar.mark()
        wst = ar.b16(1024)
        bwst = Buf()

        ck, sk = ("c1k", "s1k") if L1 else ("c0k", "s0k")
        cq, sqn = ("c1q", "s1q") if L1 else ("c0q", "s0q")

        wlist = [(w1q, wqT, bwqT), (w1k, wkT, bwkT), (w1v, wvT, bwvT),
                 (w1o, woT, bwoT)] if L1 else \
                [(w0q, wqT, bwqT), (w0k, wkT, bwkT), (w0v, wvT, bwvT)]
        for whbm, wT, bwT in wlist:
            for hf in range(8):
                ld("gpsimd", wst, whbm.rearrange("(a p) d -> p a d", p=128)[:, hf, :],
                   bwst)
                for j in range(DC):
                    pb, pbb = (ptrB, bT) if j % 2 == 0 else (psmB, bM)
                    t_op(lambda e, j=j, pb=pb: e.transpose(
                        pb[:, 0:128], wst[:, j * 128:j * 128 + 128], identb),
                        reads=(bwst, bid), writes=(pbb,))
                    v_op(lambda e, j=j, hf=hf, pb=pb, wT=wT: e.tensor_copy(
                        wT[:, j * 1024 + hf * 128:j * 1024 + (hf + 1) * 128],
                        pb[:, 0:128]),
                        reads=(pbb,), writes=(bwT,))
        ar.release(m_stage)

        if L1:
            for h in range(NH):
                ld("sync", kaug[h][64:68, :], kaug_h[:], kaug_b[h])

        def ln_to_fm(sl, p, t, xa, xbuf):
            ln_apply_pre(lnkey, t, xn16[p], bxn[p], xa, xbuf)
            fns = []
            for j in range(DC):
                def f(e, j=j, p=p):
                    return e.transpose(ptrB[:, j * 128:(j + 1) * 128],
                                       xn16[p][:, j * 128:(j + 1) * 128], identb)
                fns.append(f)
            bb.group("tensor", fns, reads=(bxn[p], bid), writes=(bT,))
            v_op(lambda e, p=p: e.tensor_copy(
                xnfm[p].rearrange("p (a c) -> p a c", c=128),
                ptrB[:, 0:1024].rearrange("p (a c) -> p a c", c=128)),
                reads=(bT,), writes=(bxnfm[p],))

        def proj(p, wT, bwT, pd, bpd):
            fns = []
            for j in range(DC):
                for hh in range(2):
                    def f(e, j=j, hh=hh, wT=wT, pd=pd, p=p):
                        return e.matmul(pd[:, hh * 512:(hh + 1) * 512],
                                        xnfm[p][:, j * 128:(j + 1) * 128],
                                        wT[:, j * 1024 + hh * 512:
                                           j * 1024 + (hh + 1) * 512],
                                        start=(j == 0), stop=(j == DC - 1))
                    fns.append(f)
            bb.group("tensor", fns, reads=(bxnfm[p], bwT), writes=(bpd,))

        def rope(p, pd, bpd, ca, sa, bca, bsa):
            r0 = rsc[p][:, 0:512]; r1 = rsc[p][:, 512:1024]
            rb = brsc[p]; ro = brot[p]; rt = rot16[p]
            if L1:
                i0, i1 = pd[:, 0:512], pd[:, 512:1024]
                o0, o1 = rt[:, 0:512], rt[:, 512:1024]
                v_op(lambda e: e.tensor_mul(r0, i0, ca), reads=(bpd, bca), writes=(rb,))
                v_op(lambda e: e.tensor_mul(r1, i1, sa), reads=(bpd, bsa), writes=(rb,))
                v_op(lambda e: e.tensor_sub(o0, r0, r1), reads=(rb,), writes=(ro,))
                v_op(lambda e: e.tensor_mul(r0, i0, sa), reads=(bpd, bsa), writes=(rb,))
                v_op(lambda e: e.tensor_mul(r1, i1, ca), reads=(bpd, bca), writes=(rb,))
                v_op(lambda e: e.tensor_add(o1, r0, r1), reads=(rb,), writes=(ro,))
            else:
                te, to = pd[:, 0::2], pd[:, 1::2]
                oe, oo = rt[:, 0::2], rt[:, 1::2]
                v_op(lambda e: e.tensor_mul(r0, te, ca), reads=(bpd, bca), writes=(rb,))
                v_op(lambda e: e.tensor_mul(r1, to, sa), reads=(bpd, bsa), writes=(rb,))
                v_op(lambda e: e.tensor_add(oe, r0, r1), reads=(rb,), writes=(ro,))
                v_op(lambda e: e.tensor_mul(r0, to, ca), reads=(bpd, bca), writes=(rb,))
                v_op(lambda e: e.tensor_mul(r1, te, sa), reads=(bpd, bsa), writes=(rb,))
                v_op(lambda e: e.tensor_sub(oo, r0, r1), reads=(rb,), writes=(ro,))

        for tc in range(NT):
            p = tc % 2
            sl = SL[p]
            if tc == 0:
                ld("sync", tabc[0], t3[ck][0], btc[0])
                ld("sync", tabs_[0], t3[sk][0], bts[0])
            if tc + 1 < NT:
                ld("sync", tabc[(tc + 1) % 2], t3[ck][tc + 1], btc[(tc + 1) % 2])
                ld("sync", tabs_[(tc + 1) % 2], t3[sk][tc + 1], bts[(tc + 1) % 2])
            xa, xbuf = get_x(tc, "kv")
            ln_to_fm(sl, p, tc, xa, xbuf)
            proj(p, wkT, bwkT, psb, bG)
            proj(p, wvT, bwvT, psA, bA)
            rope(p, psb, bG, tabc[p], tabs_[p], btc[p], bts[p])
            if L1:
                v_op(lambda e, p=p: e.scalar_tensor_tensor(
                    rsc[p][:, 0:1024], rot16[p], 1.0, rot16[p],
                    op0=ALU.mult, op1=ALU.mult),
                    reads=(brot[p],), writes=(brsc[p],))
                v_op(lambda e, p=p: e.tensor_reduce(
                    kn_t, rsc[p][:, 0:1024].rearrange("p (h d) -> p h d", d=HD),
                    axis=AX, op=ALU.add),
                    reads=(brsc[p],), writes=(bknt,))
                if tc == 0:
                    v_op(lambda e: e.tensor_copy(kn_run, kn_t), reads=(bknt,), writes=(bknr,))
                else:
                    v_op(lambda e: e.tensor_max(kn_run, kn_run, kn_t),
                         reads=(bknt, bknr), writes=(bknr,))
            if L1:
                for h in range(NH - 1):
                    pb, pbb = (ptrB, bT) if h % 2 == 0 else (psmB, bM)
                    t_op(lambda e, h=h, pb=pb, p=p: e.transpose(
                        pb[:, 0:128], rot16[p][:, h * 64:h * 64 + 128], identb),
                        reads=(brot[p], bid), writes=(pbb,))
                    v_op(lambda e, h=h, pb=pb, tc=tc: e.tensor_copy(
                        kaug[h][0:64, tc * 128:(tc + 1) * 128], pb[0:64, 0:128]),
                        reads=(pbb,), writes=(kaug_b[h],))
                    if h == NH - 2:
                        v_op(lambda e, pb=pb, tc=tc: e.tensor_copy(
                            kaug[NH - 1][0:64, tc * 128:(tc + 1) * 128],
                            pb[64:128, 0:128]),
                            reads=(pbb,), writes=(kaug_b[NH - 1],))
            else:
                fns = []
                for j in range(DC):
                    def f(e, j=j, p=p):
                        return e.transpose(ptrB[:, j * 128:(j + 1) * 128],
                                           rot16[p][:, j * 128:(j + 1) * 128], identb)
                    fns.append(f)
                bb.group("tensor", fns, reads=(brot[p], bid), writes=(bT,))
                dsel = kfm.rearrange("p (a s) -> p a s", s=S)[:, :, tc * 128:(tc + 1) * 128]
                v_op(lambda e, dsel=dsel: e.tensor_copy(
                    dsel, ptrB[:, 0:1024].rearrange("p (a c) -> p a c", c=128)),
                    reads=(bT,), writes=(bkfm,))
            v_op(lambda e, tc=tc: e.tensor_copy(vtm[:, tc * D:(tc + 1) * D], psA),
                 reads=(bA,), writes=(bvtm,))

        if L1:
            t_op(lambda e: e.transpose(psmF[0:16, 0:128], kn_run, identf),
                 reads=(bknr, bidf), writes=(bM,))
            v_op(lambda e: e.reduce_max(v_mk[0:16, 0:1], psmF[0:16, 0:128], axis=AX),
                 reads=(bM,), writes=(bmk,))
            t_op(lambda e: e.transpose(ptrF[0:1, 0:128], v_mk[0:16, 0:1], identf[0:16, :]),
                 reads=(bmk, bidf), writes=(bT,))
            v_op(lambda e: e.tensor_copy(v_mk[0:1, 0:16], ptrF[0:1, 0:16]),
                 reads=(bT,), writes=(bmk,))
            t_op(lambda e: e.matmul(psmF[:, 0:16], ones_s, v_mk[0:1, 0:16],
                                    start=True, stop=True),
                 reads=(bmk, bone), writes=(bM,))
            v_op(lambda e: e.tensor_copy(mkb_s, psmF[:, 0:16]), reads=(bM,), writes=(bmkb,))

        ar.release(m_kv)
        bb.set_guard()
        P_sb = [ar.b16(S), ar.b16(S)]
        bP = [Buf(), Buf()]
        PT_sb = ar.b16(S)
        bPT = Buf()
        tt_sb = ar.f32(1024)
        btt = Buf()
        if L1:
            qaugs = [ar.b16(NH * 128, parts=68), ar.b16(NH * 128, parts=68)]
            bqaug = [Buf(), Buf()]
            ld("sync", qaugs[0][64:68, :], qaug_h[:], bqaug[0])
            ld("sync", qaugs[1][64:68, :], qaug_h[:], bqaug[1])
            avs = ar.b16(8 * 128)
            bavs = Buf()
            o2s = ar.f32(1024)
            bo2 = Buf()
            outb_s = ar.f32(1024)
            boutb = Buf()
            ld("sync", outb_s, outbr[:], boutb)
        else:
            qfm = [ar.b16(DC * 128), ar.b16(DC * 128)]
            bqfm = [Buf(), Buf()]

        def stage_a(qt, midwork=None):
            p = qt % 2
            sl = SL[p]
            if qt == 0:
                ld("sync", tabc[0], t3[cq][0], btc[0])
                ld("sync", tabs_[0], t3[sqn][0], bts[0])
            if qt + 1 < NT:
                ld("sync", tabc[(qt + 1) % 2], t3[cq][qt + 1], btc[(qt + 1) % 2])
                ld("sync", tabs_[(qt + 1) % 2], t3[sqn][qt + 1], bts[(qt + 1) % 2])
            xa, xbuf = get_x(qt, "q")
            ln_to_fm(sl, p, qt, xa, xbuf)
            proj(p, wqT, bwqT, psb, bG)
            rope(p, psb, bG, tabc[p], tabs_[p], btc[p], bts[p])
            if midwork is not None:
                midwork()
            if L1:
                v_op(lambda e, p=p: e.scalar_tensor_tensor(
                    rsc[p][:, 0:1024], rot16[p], 1.0, rot16[p],
                    op0=ALU.mult, op1=ALU.mult),
                    reads=(brot[p],), writes=(brsc[p],))
                v_op(lambda e, p=p: e.tensor_reduce(
                    qn_t, rsc[p][:, 0:1024].rearrange("p (h d) -> p h d", d=HD),
                    axis=AX, op=ALU.add),
                    reads=(brsc[p],), writes=(bqnt,))
                v_op(lambda e, sl=sl: e.tensor_mul(sl.beta, qn_t, mkb_s),
                     reads=(bqnt, bmkb), writes=(sl.bbeta,))
                s_op(lambda e, sl=sl: e.activation(out=sl.beta, in_=sl.beta,
                                                   func=ACT.Ln, scale=1.0),
                     reads=(sl.bbeta,), writes=(sl.bbeta,))
                s_op(lambda e, sl=sl: e.activation(out=sl.beta, in_=sl.beta,
                                                   func=ACT.Exp, scale=0.5),
                     reads=(sl.bbeta,), writes=(sl.bbeta,))
                v_op(lambda e, qt=qt, sl=sl: e.scalar_tensor_tensor(
                    sl.beta, sl.beta, -1.0, qsl_s[:, qt * NH:(qt + 1) * NH],
                    op0=ALU.mult, op1=ALU.add),
                    reads=(sl.bbeta, bqsl), writes=(sl.bbeta,))
                for h in range(NH - 1):
                    pb, pbb = (ptrB, bT) if h % 2 == 0 else (psmB, bM)
                    t_op(lambda e, h=h, pb=pb, p=p: e.transpose(
                        pb[:, 0:128], rot16[p][:, h * 64:h * 64 + 128], identb),
                        reads=(brot[p], bid), writes=(pbb,))
                    v_op(lambda e, h=h, pb=pb, p=p: e.tensor_copy(
                        qaugs[p][0:64, h * 128:(h + 1) * 128], pb[0:64, 0:128]),
                        reads=(pbb,), writes=(bqaug[p],))
                    if h == NH - 2:
                        v_op(lambda e, pb=pb, p=p: e.tensor_copy(
                            qaugs[p][0:64, (NH - 1) * 128:NH * 128], pb[64:128, 0:128]),
                            reads=(pbb,), writes=(bqaug[p],))
            else:
                fns = []
                for j in range(DC):
                    def f(e, j=j, p=p):
                        return e.transpose(ptrB[:, j * 128:(j + 1) * 128],
                                           rot16[p][:, j * 128:(j + 1) * 128], identb)
                    fns.append(f)
                bb.group("tensor", fns, reads=(brot[p], bid), writes=(bT,))
                v_op(lambda e, p=p: e.tensor_copy(
                    qfm[p].rearrange("p (a c) -> p a c", c=128),
                    ptrB[:, 0:1024].rearrange("p (a c) -> p a c", c=128)),
                    reads=(bT,), writes=(bqfm[p],))
            return xa, xbuf

        def segs_of(span):
            out = [(0, min(span, 1024), psA, bA)]
            if span > 1024:
                out.append((1024, span, psB, bB))
            return out

        def flush_pt_av(qt, h, hp):
            span = (qt + 1) * 128
            nkb = qt + 1
            for r0 in range(0, nkb, 8):
                r1 = min(r0 + 8, nkb)
                pb, pbb = (ptrB, bT) if (r0 // 8) % 2 == 0 else (psmB, bM)
                fns = []
                for kb in range(r0, r1):
                    def f(e, kb=kb, r0=r0, pb=pb, hp=hp):
                        return e.transpose(
                            pb[:, (kb - r0) * 128:(kb - r0 + 1) * 128],
                            P_sb[hp][:, kb * 128:(kb + 1) * 128], identb)
                    fns.append(f)
                bb.group("tensor", fns, reads=(bP[hp], bid), writes=(pbb,))
                nb = r1 - r0
                v_op(lambda e, r0=r0, nb=nb, pb=pb: e.tensor_copy(
                    PT_sb[:, r0 * 128:(r0 + nb) * 128].rearrange(
                        "p (a c) -> p a c", c=128),
                    pb[:, 0:nb * 128].rearrange("p (a c) -> p a c", c=128)),
                    reads=(pbb,), writes=(bPT,))
            if L1:
                pr, po = h // 2, (h % 2) * 64
                fns = []
                for kb in range(nkb):
                    def f(e, kb=kb, h=h, pr=pr, po=po):
                        return e.matmul(
                            psb[po:po + 64, pr * 128:(pr + 1) * 128],
                            vtm[:, kb * D + h * HD:kb * D + (h + 1) * HD],
                            PT_sb[:, kb * 128:(kb + 1) * 128],
                            start=(kb == 0), stop=(kb == nkb - 1),
                            tile_position=(0, po))
                    fns.append(f)
                bb.group("tensor", fns, reads=(bvtm, bPT), writes=(bG,))
            else:
                fns = []
                for kb in range(nkb):
                    for hh in range(2):
                        def f(e, kb=kb, hh=hh):
                            return e.matmul(
                                psb[:, hh * 512:(hh + 1) * 512],
                                PT_sb[:, kb * 128:(kb + 1) * 128],
                                vtm[:, kb * D + hh * 512:kb * D + (hh + 1) * 512],
                                start=(kb == 0), stop=(kb == nkb - 1))
                        fns.append(f)
                bb.group("tensor", fns, reads=(bPT, bvtm), writes=(bG,))

        def l0_epilogue(qt, xa, xbuf, lcol_info):
            sl = SL[qt % 2]
            lc_ap, lc_buf = lcol_info
            stats(sl, psb, bG)
            v_op(lambda e, sl=sl: e.tensor_mul(sl.l2, lc_ap, lc_ap),
                 reads=(lc_buf,), writes=(sl.bl2,))
            v_op(lambda e, sl=sl: e.reciprocal(sl.l2, sl.l2),
                 reads=(sl.bl2,), writes=(sl.bl2,))
            v_op(lambda e, sl=sl: e.scalar_tensor_tensor(
                sl.tmp, sl.l2, 1e-5, sl.mv[:, 1:2], op0=ALU.mult, op1=ALU.add),
                reads=(sl.bl2, sl.bmv), writes=(sl.btmp,))
            s_op(lambda e, sl=sl: e.activation(out=sl.l2, in_=sl.tmp, func=ACT.Ln,
                                               scale=1.0),
                 reads=(sl.btmp,), writes=(sl.bl2,))
            s_op(lambda e, sl=sl: e.activation(out=sl.sg, in_=sl.l2, func=ACT.Exp,
                                               scale=-0.5),
                 reads=(sl.bl2,), writes=(sl.bsg,))
            v_op(lambda e, sl=sl: e.tensor_scalar(sl.ngm, sl.mv[:, 0:1], sl.sg, -1.0,
                                                  op0=ALU.mult, op1=ALU.mult),
                 reads=(sl.bmv, sl.bsg), writes=(sl.bngm,))
            s_op(lambda e, sl=sl: e.activation(out=tt_sb, in_=psb, func=ACT.Identity,
                                               scale=sl.sg, bias=sl.ngm),
                 reads=(bG, sl.bsg, sl.bngm), writes=(btt,))
            tok = v_op(lambda e, xa=xa: e.tensor_add(xa, xa, tt_sb),
                       reads=(btt,), writes=(xbuf,))
            if lnout is not None:
                ln_precompute(lnout, qt, xa, xbuf, sl)
            put_x(qt, tok, xa, xbuf)

        def l1_av_evac():
            v_op(lambda e: e.tensor_copy(avs, psb), reads=(bG,), writes=(bavs,))

        def l1_epilogue(qt, xa, xbuf):
            sl = SL[qt % 2]
            fns = []
            for pr in range(8):
                for hh in range(2):
                    def f(e, pr=pr, hh=hh):
                        return e.matmul(psA[:, hh * 512:(hh + 1) * 512],
                                        avs[:, pr * 128:(pr + 1) * 128],
                                        woT[:, pr * 1024 + hh * 512:
                                            pr * 1024 + (hh + 1) * 512],
                                        start=(pr == 0), stop=(pr == 7))
                    fns.append(f)
            bb.group("tensor", fns, reads=(bavs, bwoT), writes=(bA,))
            v_op(lambda e: e.tensor_add(o2s, psA, outb_s),
                 reads=(bA, boutb), writes=(bo2,))
            stats(sl, o2s, bo2)
            rsqrt_chain(sl, sl.sg, sl.bsg, sl.mv[:, 1:2], sl.bmv, 1.0, 0)
            v_op(lambda e, sl=sl: e.tensor_scalar(sl.ngm, sl.mv[:, 0:1], sl.sg, -1.0,
                                                  op0=ALU.mult, op1=ALU.mult),
                 reads=(sl.bmv, sl.bsg), writes=(sl.bngm,))
            s_op(lambda e, sl=sl: e.activation(out=tt_sb, in_=o2s, func=ACT.Identity,
                                               scale=sl.sg, bias=sl.ngm),
                 reads=(bo2, sl.bsg, sl.bngm), writes=(btt,))
            tok = v_op(lambda e, xa=xa: e.tensor_add(xa, xa, tt_sb),
                       reads=(btt,), writes=(xbuf,))
            if lnout is not None:
                ln_precompute(lnout, qt, xa, xbuf, sl,
                              var_scale=1.0 + 1e-3, eps_col=1)
            put_x(qt, tok, xa, xbuf)

        if L1:
            xe = [ar.f32(1024), ar.f32(1024)]
            xeb = [Buf(), Buf()]
            pend = [None]
            pend_ep = [None]

            def flush_seam():
                if pend_ep[0] is not None:
                    q_ = pend_ep[0]
                    ld("sync", xe[q_ % 2], xsp3[q_], xeb[q_ % 2])
                    l1_epilogue(q_, xe[q_ % 2], xeb[q_ % 2])
                    pend_ep[0] = None

            for qt in range(NT):
                p = qt % 2
                sl = SL[p]
                xa, xbuf = stage_a(qt, midwork=flush_seam)
                span = (qt + 1) * 128
                segs = segs_of(span)
                for h in range(NH):
                    hp = h % 2
                    use_segs = segs
                    if span <= 1024:
                        ps_, bps_ = (psA, bA) if hp == 0 else (psB, bB)
                        use_segs = [(0, span, ps_, bps_)]
                    for (lo, hi, ps, bps) in use_segs:
                        fns = []
                        for c0 in range(lo, hi, 512):
                            c1 = min(c0 + 512, hi)
                            def f(e, h=h, c0=c0, c1=c1, ps=ps, lo=lo, p=p):
                                return e.matmul(ps[:, c0 - lo:c1 - lo],
                                                qaugs[p][:, h * 128:(h + 1) * 128],
                                                kaug[h][:, c0:c1],
                                                start=True, stop=True)
                            fns.append(f)
                        bb.group("tensor", fns, reads=(bqaug[p], kaug_b[h]), writes=(bps,))
                    dlo = span - 128
                    mseg = use_segs[-1]
                    mlo = mseg[0]
                    msp, msb = mseg[2], mseg[3]
                    v_op(lambda e, dlo=dlo, msp=msp, mlo=mlo: e.tensor_add(
                        msp[:, dlo - mlo:dlo - mlo + 128],
                        msp[:, dlo - mlo:dlo - mlo + 128], mask_s),
                        reads=(bmsk,), writes=(msb,))
                    if pend[0] is not None:
                        q_, h_, hp_ = pend[0]
                        flush_pt_av(q_, h_, hp_)
                        pend[0] = None
                    bcol = sl.beta[:, h:h + 1]
                    hsl = SL[hp]
                    for gi, (lo, hi, ps, bps) in enumerate(use_segs):
                        s_op(lambda e, lo=lo, hi=hi, ps=ps, gi=gi, bcol=bcol, hp=hp,
                             hsl=hsl: e.activation(
                                 out=P_sb[hp][:, lo:hi], in_=ps[:, 0:hi - lo],
                                 func=ACT.Exp, scale=1.0, bias=bcol,
                                 accum_out=hsl.lp[:, gi:gi + 1]),
                             reads=(bps, sl.bbeta), writes=(bP[hp], hsl.blp))
                    if len(use_segs) == 1:
                        v_op(lambda e, hsl=hsl: e.reciprocal(hsl.lc, hsl.lp[:, 0:1]),
                             reads=(hsl.blp,), writes=(hsl.blc,))
                    else:
                        v_op(lambda e, hsl=hsl: e.tensor_add(hsl.lc, hsl.lp[:, 0:1],
                                                             hsl.lp[:, 1:2]),
                             reads=(hsl.blp,), writes=(hsl.blc,))
                        v_op(lambda e, hsl=hsl: e.reciprocal(hsl.lc, hsl.lc),
                             reads=(hsl.blc,), writes=(hsl.blc,))
                    v_op(lambda e, span=span, hp=hp, hsl=hsl: e.tensor_scalar(
                        P_sb[hp][:, 0:span], P_sb[hp][:, 0:span], hsl.lc, None,
                        op0=ALU.mult),
                        reads=(hsl.blc,), writes=(bP[hp],))
                    pend[0] = (qt, h, hp)
                q_, h_, hp_ = pend[0]
                flush_pt_av(q_, h_, hp_)
                pend[0] = None
                l1_av_evac()
                pend_ep[0] = qt
            flush_seam()
        else:
            pend = [None]

            def flush_seam0():
                if pend[0] is not None:
                    q_, xa_, xb_, lci = pend[0]
                    flush_pt_av(q_, 0, q_ % 2)
                    l0_epilogue(q_, xa_, xb_, lci)
                    pend[0] = None

            for qt in range(NT):
                p = qt % 2
                sl = SL[p]
                xa, xbuf = stage_a(qt, midwork=flush_seam0)
                span = (qt + 1) * 128
                segs = segs_of(span)
                for (lo, hi, ps, bps) in segs:
                    for c0 in range(lo, hi, 512):
                        c1 = min(c0 + 512, hi)
                        fns = []
                        for j in range(DC):
                            def f(e, j=j, c0=c0, c1=c1, ps=ps, lo=lo, p=p):
                                return e.matmul(ps[:, c0 - lo:c1 - lo],
                                                qfm[p][:, j * 128:(j + 1) * 128],
                                                kfm[:, j * S + c0:j * S + c1],
                                                start=(j == 0), stop=(j == DC - 1))
                            fns.append(f)
                        bb.group("tensor", fns, reads=(bqfm[p], bkfm), writes=(bps,))
                dlo = span - 128
                ps, bps, off = (psA, bA, 0) if dlo < 1024 else (psB, bB, 1024)
                v_op(lambda e, dlo=dlo, ps=ps, off=off: e.tensor_add(
                    ps[:, dlo - off:dlo - off + 128],
                    ps[:, dlo - off:dlo - off + 128], mask_s),
                    reads=(bmsk,), writes=(bps,))
                for gi, (lo, hi, ps2, bps2) in enumerate(segs):
                    v_op(lambda e, ps2=ps2, hi=hi, lo=lo, gi=gi, sl=sl: e.reduce_max(
                        sl.lp[:, 2 + gi:3 + gi], ps2[:, 0:hi - lo], axis=AX),
                        reads=(bps2,), writes=(sl.blp,))
                if len(segs) == 2:
                    v_op(lambda e, sl=sl: e.tensor_max(sl.lp[:, 2:3], sl.lp[:, 2:3],
                                                       sl.lp[:, 3:4]),
                         reads=(sl.blp,), writes=(sl.blp,))
                v_op(lambda e, sl=sl: e.tensor_scalar(sl.beta[:, 0:1], sl.lp[:, 2:3],
                                                      -1.0, None, op0=ALU.mult),
                     reads=(sl.blp,), writes=(sl.bbeta,))
                for gi, (lo, hi, ps, bps) in enumerate(segs):
                    s_op(lambda e, lo=lo, hi=hi, ps=ps, gi=gi, sl=sl, p=p:
                         e.activation(out=P_sb[p][:, lo:hi], in_=ps[:, 0:hi - lo],
                                      func=ACT.Exp, scale=1.0, bias=sl.beta[:, 0:1],
                                      accum_out=sl.lp[:, gi:gi + 1]),
                         reads=(bps, sl.bbeta), writes=(bP[p], sl.blp))
                if len(segs) == 1:
                    v_op(lambda e, sl=sl: e.reciprocal(sl.lc, sl.lp[:, 0:1]),
                         reads=(sl.blp,), writes=(sl.blc,))
                else:
                    v_op(lambda e, sl=sl: e.tensor_add(sl.lc, sl.lp[:, 0:1],
                                                       sl.lp[:, 1:2]),
                         reads=(sl.blp,), writes=(sl.blc,))
                    v_op(lambda e, sl=sl: e.reciprocal(sl.lc, sl.lc),
                         reads=(sl.blc,), writes=(sl.blc,))
                pend[0] = (qt, xa, xbuf, (sl.lc, sl.blc))
            flush_seam0()
        ar.release(m0)

    def attention1_st(get_x, put_x):
        m0 = ar.mark()
        kaug = [ar.b16(S, parts=69) for _ in range(NH)]
        kaug_b = [Buf() for _ in range(NH)]
        vtm = ar.b16(NT * NH * 65)
        bvtm = Buf()
        woT = ar.b16(DC * D)
        bwoT = Buf()
        betaT = ar.b16(S, parts=16)
        bbetaT = Buf()

        m_q0 = ar.mark()
        wqT = ar.b16(DC * D)
        bwqT = Buf()
        xn16 = [ar.b16(D), ar.b16(D)]
        bxn = [Buf(), Buf()]
        xnfm = [ar.b16(DC * 128), ar.b16(DC * 128)]
        bxnfm = [Buf(), Buf()]
        rsc0 = ar.f32(1024)
        rsc = [rsc0, rsc0]
        brsc0 = Buf()
        brsc = [brsc0, brsc0]
        rot16 = [ar.b16(D), ar.b16(D)]
        brot = [Buf(), Buf()]
        tabc = [ar.f32(512), ar.f32(512)]
        tabs_ = [ar.f32(512), ar.f32(512)]
        btc = [Buf(), Buf()]; bts = [Buf(), Buf()]
        m_kv = ar.mark()
        wkT = ar.b16(DC * D)
        bwkT = Buf()
        wvT = ar.b16(DC * D)
        bwvT = Buf()
        m_stage = ar.mark()
        wst = ar.b16(1024)
        bwst = Buf()

        for whbm, wT, bwT in ((w1q, wqT, bwqT), (w1k, wkT, bwkT),
                              (w1v, wvT, bwvT), (w1o, woT, bwoT)):
            for hf in range(8):
                ld("gpsimd", wst, whbm.rearrange("(a p) d -> p a d", p=128)[:, hf, :],
                   bwst)
                for j in range(DC):
                    pb, pbb = (ptrB, bT) if j % 2 == 0 else (psmB, bM)
                    t_op(lambda e, j=j, pb=pb: e.transpose(
                        pb[:, 0:128], wst[:, j * 128:j * 128 + 128], identb),
                        reads=(bwst, bid), writes=(pbb,))
                    v_op(lambda e, j=j, hf=hf, pb=pb, wT=wT: e.tensor_copy(
                        wT[:, j * 1024 + hf * 128:j * 1024 + (hf + 1) * 128],
                        pb[:, 0:128]),
                        reads=(pbb,), writes=(bwT,))
        for h in range(NH):
            ld("sync", kaug[h][64:69, :], kaug_h[:], kaug_b[h])
        ld("sync", wst[:, 0:256], onesb_h[:], bwst)
        vtm4 = vtm.rearrange("p (t h c) -> p t h c", t=NT, h=NH)
        v_op(lambda e: e.tensor_copy(
            vtm4[:, :, :, 64:65],
            wst[:, 0:256].rearrange("p (t h c) -> p t h c", t=NT, c=1)),
            reads=(bwst,), writes=(bvtm,))
        ar.release(m_stage)

        def ln_to_fm(p, t, xa, xbuf):
            ln_apply_pre("l1", t, xn16[p], bxn[p], xa, xbuf)
            fns = []
            for j in range(DC):
                def f(e, j=j, p=p):
                    return e.transpose(ptrB[:, j * 128:(j + 1) * 128],
                                       xn16[p][:, j * 128:(j + 1) * 128], identb)
                fns.append(f)
            bb.group("tensor", fns, reads=(bxn[p], bid), writes=(bT,))
            v_op(lambda e, p=p: e.tensor_copy(
                xnfm[p].rearrange("p (a c) -> p a c", c=128),
                ptrB[:, 0:1024].rearrange("p (a c) -> p a c", c=128)),
                reads=(bT,), writes=(bxnfm[p],))

        def proj(p, wT, bwT, pd, bpd):
            fns = []
            for j in range(DC):
                for hh in range(2):
                    def f(e, j=j, hh=hh, wT=wT, pd=pd, p=p):
                        return e.matmul(pd[:, hh * 512:(hh + 1) * 512],
                                        xnfm[p][:, j * 128:(j + 1) * 128],
                                        wT[:, j * 1024 + hh * 512:
                                           j * 1024 + (hh + 1) * 512],
                                        start=(j == 0), stop=(j == DC - 1))
                    fns.append(f)
            bb.group("tensor", fns, reads=(bxnfm[p], bwT), writes=(bpd,))

        def rope(p, pd, bpd):
            r0 = rsc[p][:, 0:512]; r1 = rsc[p][:, 512:1024]
            rb = brsc[p]; ro = brot[p]; rt = rot16[p]
            ca, sa = tabc[p], tabs_[p]
            bca, bsa = btc[p], bts[p]
            i0, i1 = pd[:, 0:512], pd[:, 512:1024]
            o0, o1 = rt[:, 0:512], rt[:, 512:1024]
            v_op(lambda e: e.tensor_mul(r0, i0, ca), reads=(bpd, bca), writes=(rb,))
            v_op(lambda e: e.tensor_mul(r1, i1, sa), reads=(bpd, bsa), writes=(rb,))
            v_op(lambda e: e.tensor_sub(o0, r0, r1), reads=(rb,), writes=(ro,))
            v_op(lambda e: e.tensor_mul(r0, i0, sa), reads=(bpd, bsa), writes=(rb,))
            v_op(lambda e: e.tensor_mul(r1, i1, ca), reads=(bpd, bca), writes=(rb,))
            v_op(lambda e: e.tensor_add(o1, r0, r1), reads=(rb,), writes=(ro,))

        def headsplit(p, dsts, dbufs):
            for h in range(NH - 1):
                pb, pbb = (ptrB, bT) if h % 2 == 0 else (psmB, bM)
                t_op(lambda e, h=h, pb=pb, p=p: e.transpose(
                    pb[:, 0:128], rot16[p][:, h * 64:h * 64 + 128], identb),
                    reads=(brot[p], bid), writes=(pbb,))
                v_op(lambda e, h=h, pb=pb: e.tensor_copy(
                    dsts(h), pb[0:64, 0:128]),
                    reads=(pbb,), writes=(dbufs(h),))
                if h == NH - 2:
                    v_op(lambda e, pb=pb: e.tensor_copy(
                        dsts(NH - 1), pb[64:128, 0:128]),
                        reads=(pbb,), writes=(dbufs(NH - 1),))

        for tc in range(NT):
            p = tc % 2
            if tc == 0:
                ld("sync", tabc[0], t3["c1k"][0], btc[0])
                ld("sync", tabs_[0], t3["s1k"][0], bts[0])
            if tc + 1 < NT:
                ld("sync", tabc[(tc + 1) % 2], t3["c1k"][tc + 1], btc[(tc + 1) % 2])
                ld("sync", tabs_[(tc + 1) % 2], t3["s1k"][tc + 1], bts[(tc + 1) % 2])
            xa, xbuf = get_x(tc, "kv")
            ln_to_fm(p, tc, xa, xbuf)
            proj(p, wkT, bwkT, psb, bG)
            proj(p, wvT, bwvT, psA, bA)
            rope(p, psb, bG)
            v_op(lambda e, p=p: e.scalar_tensor_tensor(
                rsc[p][:, 0:1024], rot16[p], 1.0, rot16[p],
                op0=ALU.mult, op1=ALU.mult),
                reads=(brot[p],), writes=(brsc[p],))
            v_op(lambda e, p=p: e.tensor_reduce(
                kn_t, rsc[p][:, 0:1024].rearrange("p (h d) -> p h d", d=HD),
                axis=AX, op=ALU.add),
                reads=(brsc[p],), writes=(bknt,))
            if tc == 0:
                v_op(lambda e: e.tensor_copy(kn_run, kn_t), reads=(bknt,), writes=(bknr,))
            else:
                v_op(lambda e: e.tensor_max(kn_run, kn_run, kn_t),
                     reads=(bknt, bknr), writes=(bknr,))
            headsplit(p, lambda h, tc=tc: kaug[h][0:64, tc * 128:(tc + 1) * 128],
                      lambda h: kaug_b[h])
            v_op(lambda e, tc=tc: e.tensor_copy(
                vtm4[:, tc, :, 0:64],
                psA.rearrange("p (h c) -> p h c", c=64)),
                reads=(bA,), writes=(bvtm,))

        t_op(lambda e: e.transpose(psmF[0:16, 0:128], kn_run, identf),
             reads=(bknr, bidf), writes=(bM,))
        v_op(lambda e: e.reduce_max(v_mk[0:16, 0:1], psmF[0:16, 0:128], axis=AX),
             reads=(bM,), writes=(bmk,))
        t_op(lambda e: e.transpose(ptrF[0:1, 0:128], v_mk[0:16, 0:1], identf[0:16, :]),
             reads=(bmk, bidf), writes=(bT,))
        v_op(lambda e: e.tensor_copy(v_mk[0:1, 0:16], ptrF[0:1, 0:16]),
             reads=(bT,), writes=(bmk,))
        t_op(lambda e: e.matmul(psmF[:, 0:16], ones_s, v_mk[0:1, 0:16],
                                start=True, stop=True),
             reads=(bmk, bone), writes=(bM,))
        v_op(lambda e: e.tensor_copy(mkb_s, psmF[:, 0:16]), reads=(bM,), writes=(bmkb,))
        bb.set_guard()

        if _DBG == "kv":
            toks = []
            for i, hh_ in enumerate((0, 15)):
                toks.append(bb.op("gpsimd", lambda e, i=i, hh_=hh_: e.dma_start(
                    out=out3[i][0:69, :], in_=kaug[hh_].bitcast(F32)),
                    reads=(kaug_b[hh_],), dma=True))
            vtf = vtm.bitcast(F32)
            for t in range(8):
                toks.append(bb.op("gpsimd", lambda e, t=t: e.dma_start(
                    out=out3[2 + t], in_=vtf[:, t * 1024:(t + 1) * 1024]),
                    reads=(bvtm,), dma=True))
            bb.op("gpsimd", lambda e: e.nop(), extra=tuple(toks))
            return "dbg"
        ar.release(m_kv)

        qaugs = [ar.b16(NH * 128, parts=69), ar.b16(NH * 128, parts=69)]
        bqaug = [Buf(), Buf()]
        beta16 = ar.b16(16)
        bbeta16 = Buf()
        for p_ in range(2):
            ld("sync", qaugs[p_][64:69, :], qaug_h[:], bqaug[p_])
        QOPS = int(_os.environ.get("KQOPS", 99))
        for qt in range(int(_os.environ.get("KQN", NT))):
            p = qt % 2
            sl = SL[p]
            if qt == 0:
                ld("sync", tabc[0], t3["c1q"][0], btc[0])
                ld("sync", tabs_[0], t3["s1q"][0], bts[0])
            if qt + 1 < NT:
                ld("sync", tabc[(qt + 1) % 2], t3["c1q"][qt + 1], btc[(qt + 1) % 2])
                ld("sync", tabs_[(qt + 1) % 2], t3["s1q"][qt + 1], bts[(qt + 1) % 2])
            xa, xbuf = get_x(qt, "q")
            if QOPS < 1:
                continue
            ln_to_fm(p, qt, xa, xbuf)
            if QOPS < 2:
                continue
            proj(p, wqT, bwqT, psb, bG)
            if QOPS < 3:
                continue
            rope(p, psb, bG)
            if QOPS < 4:
                continue
            v_op(lambda e, p=p: e.scalar_tensor_tensor(
                rsc[p][:, 0:1024], rot16[p], 1.0, rot16[p],
                op0=ALU.mult, op1=ALU.mult),
                reads=(brot[p],), writes=(brsc[p],))
            v_op(lambda e, p=p: e.tensor_reduce(
                qn_t, rsc[p][:, 0:1024].rearrange("p (h d) -> p h d", d=HD),
                axis=AX, op=ALU.add),
                reads=(brsc[p],), writes=(bqnt,))
            v_op(lambda e, sl=sl: e.tensor_mul(sl.beta, qn_t, mkb_s),
                 reads=(bqnt, bmkb), writes=(sl.bbeta,))
            s_op(lambda e, sl=sl: e.activation(out=sl.beta, in_=sl.beta,
                                               func=ACT.Ln, scale=1.0),
                 reads=(sl.bbeta,), writes=(sl.bbeta,))
            s_op(lambda e, sl=sl: e.activation(out=sl.beta, in_=sl.beta,
                                               func=ACT.Exp, scale=0.5),
                 reads=(sl.bbeta,), writes=(sl.bbeta,))
            v_op(lambda e, qt=qt, sl=sl: e.scalar_tensor_tensor(
                sl.beta, sl.beta, -1.0, qsl_s[:, qt * NH:(qt + 1) * NH],
                op0=ALU.mult, op1=ALU.add),
                reads=(sl.bbeta, bqsl), writes=(sl.bbeta,))
            if QOPS < 5:
                continue
            v_op(lambda e, sl=sl: e.tensor_copy(beta16[:, 0:16], sl.beta),
                 reads=(sl.bbeta,), writes=(bbeta16,))
            t_op(lambda e: e.transpose(ptrB[0:16, 0:128], beta16[:, 0:16], identb),
                 reads=(bbeta16, bid), writes=(bT,))
            v_op(lambda e, qt=qt: e.tensor_copy(
                betaT[0:16, qt * 128:(qt + 1) * 128], ptrB[0:16, 0:128]),
                reads=(bT,), writes=(bbetaT,))
            if QOPS < 6:
                continue
            headsplit(p, lambda h, p=p: qaugs[p][0:64, h * 128:(h + 1) * 128],
                      lambda h, p=p: bqaug[p])
            if QOPS < 7:
                continue
            bb.op("gpsimd", lambda e, qt=qt, p=p: e.dma_start(
                out=qsp[qt], in_=qaugs[p][0:69, :]),
                reads=(bqaug[p],), dma=True)
        bb.set_guard()
        if _DBG == "q":
            toks = []
            for i, hh_ in enumerate((0, 15)):
                toks.append(bb.op("gpsimd", lambda e, i=i, hh_=hh_: e.dma_start(
                    out=out3[i][0:69, :], in_=kaug[hh_].bitcast(F32)),
                    reads=(kaug_b[hh_],), extra=tuple(bb.guard), dma=True))
            bb.op("gpsimd", lambda e: e.nop(), extra=tuple(toks))
            return "dbg"

        ar.release(m_q0)

        qbuf = [ar.b16(S, parts=69), ar.b16(S, parts=69)]
        qb_b = [Buf(), Buf()]
        PTs = [ar.b16(S), ar.b16(S), ar.b16(S)]
        PT_b = [Buf(), Buf(), Buf()]
        avs = ar.b16(8 * S)
        bavs = Buf()
        rdv = [ar.f32(512)[0:1, :], ar.f32(512)[0:1, :]]
        brd = [Buf(), Buf()]
        bcs = [ar.f32(512), ar.f32(512)]
        bbcs = [Buf(), Buf()]
        avp = [(psA, 0), (psA, 512), (psB, 0), (psB, 512)]
        avp_b = [Buf(), Buf(), Buf(), Buf()]
        pSs = [(psb, 0), (psb, 512), (ptrF, 0), (psmF, 0)]
        pS_b = [Buf(), Buf(), bT, bM]
        gath = [[], []]
        last_rd = [None, None]
        csel = [0]

        def gather(h):
            p = h % 2
            toks = []
            g = (last_rd[p],) if last_rd[p] is not None else tuple(bb.guard)
            for tc in range(NT):
                toks.append(bb.op("sync", lambda e, tc=tc, h=h, p=p: e.dma_start(
                    out=qbuf[p][0:69, tc * 128:(tc + 1) * 128],
                    in_=qsp[tc][:, h * 128:(h + 1) * 128]),
                    extra=g, dma=True))
            gath[p] = toks
            bb.op("sync", lambda e, h=h, p=p: e.dma_start(
                out=qbuf[p][64:65, :], in_=betaT[h:h + 1, :]),
                reads=(bbetaT,), writes=(qb_b[p],), extra=g + tuple(toks),
                dma=True)

        last_sc = {}

        def emit_scores(h, kb):
            p = h % 2
            kbp = kb % 3
            q0 = kb * 128
            first = (kb == 0)
            off = 0
            while q0 + off < S:
                w = min(512, S - q0 - off)
                ps, po = pSs[csel[0] % 4]
                bps = pS_b[csel[0] % 4]
                csel[0] += 1
                ex = tuple(gath[p]) if first and off == 0 else ()
                last_sc[h] = t_op(
                    lambda e, h=h, kb=kb, q0=q0, off=off, w=w, ps=ps, po=po:
                    e.matmul(ps[0:128, po:po + w],
                             kaug[h][:, kb * 128:(kb + 1) * 128],
                             qbuf[h % 2][:, q0 + off:q0 + off + w],
                             start=True, stop=True),
                    reads=(kaug_b[h], qb_b[p]), writes=(bps,), extra=ex)
                if off == 0:
                    v_op(lambda e, ps=ps, po=po: e.tensor_add(
                        ps[:, po:po + 128], ps[:, po:po + 128], maskT_s),
                        reads=(bmskT,), writes=(bps,))
                s_op(lambda e, kbp=kbp, q0=q0, off=off, w=w, ps=ps, po=po:
                     e.activation(out=PTs[kbp][:, q0 + off:q0 + off + w],
                                  in_=ps[:, po:po + w], func=ACT.Exp, scale=1.0),
                     reads=(bps,), writes=(PT_b[kbp],))
                off += w

        def emit_av(h, kb):
            kbp = kb % 3
            for c in range(kb // 4, 4):
                lo = max(kb * 128, c * 512)
                hi = min(S, (c + 1) * 512)
                bank, boff = avp[c]
                t_op(lambda e, h=h, kb=kb, c=c, lo=lo, hi=hi, bank=bank, boff=boff,
                     kbp=kbp: e.matmul(
                         bank[0:65, boff + lo - c * 512:boff + hi - c * 512],
                         vtm[:, kb * 1040 + h * 65:kb * 1040 + (h + 1) * 65],
                         PTs[kbp][:, lo:hi],
                         start=(kb == 0), stop=(kb == min(15, 4 * c + 3))),
                     reads=(bvtm, PT_b[kbp]), writes=(avp_b[c],))

        def evac_chunk(h, c):
            rp = c % 2
            bank, boff = avp[c]
            pb, pbb = (ptrF, bT) if rp == 0 else (psmF, bM)
            v_op(lambda e, bank=bank, boff=boff, rp=rp: e.reciprocal(
                rdv[rp], bank[64:65, boff:boff + 512]),
                reads=(avp_b[c],), writes=(brd[rp],))
            t_op(lambda e, rp=rp, pb=pb: e.matmul(
                pb[0:64, 0:512], ones_s[0:1, 0:64], rdv[rp],
                start=True, stop=True),
                reads=(brd[rp], bone), writes=(pbb,))
            v_op(lambda e, rp=rp, pb=pb: e.tensor_copy(
                bcs[rp][0:64, :], pb[0:64, 0:512]),
                reads=(pbb,), writes=(bbcs[rp],))
            v_op(lambda e, h=h, c=c, bank=bank, boff=boff, rp=rp:
                 e.tensor_mul(
                     avs[(h % 2) * 64:(h % 2) * 64 + 64,
                         (h // 2) * S + c * 512:(h // 2) * S + (c + 1) * 512],
                     bank[0:64, boff:boff + 512], bcs[rp][0:64, :]),
                 reads=(avp_b[c], bbcs[rp]), writes=(bavs,))

        if _DBG == "pt":
            gather(0)
            for kb in range(15):
                emit_scores(0, kb)
            toks = [bb.op("gpsimd", lambda e: e.dma_start(
                out=out3[0], in_=PTs[0].bitcast(F32)),
                reads=(PT_b[0],), dma=True)]
            toks.append(bb.op("gpsimd", lambda e: e.dma_start(
                out=out3[1], in_=PTs[1].bitcast(F32)),
                reads=(PT_b[1],), dma=True))
            toks.append(bb.op("gpsimd", lambda e: e.dma_start(
                out=out3[2][0:69, :], in_=qbuf[0].bitcast(F32)),
                reads=(qb_b[0],), extra=tuple(gath[0]), dma=True))
            toks.append(bb.op("gpsimd", lambda e: e.dma_start(
                out=out3[3][0:69, :], in_=kaug[0].bitcast(F32)),
                reads=(kaug_b[0],), dma=True))
            bb.op("gpsimd", lambda e: e.nop(), extra=tuple(toks))
            return "dbg"

        PIPE = _os.environ.get("KNOPIPE", "") == ""
        gather(0)
        for h in range(NH):
            if h + 1 < NH:
                gather(h + 1)
            if PIPE:
                emit_scores(h, 0)
                emit_scores(h, 1)
                emit_scores(h, 2)
                for kb in range(NT):
                    emit_av(h, kb)
                    if kb + 3 < NT:
                        emit_scores(h, kb + 3)
                    if kb in (4, 8, 12):
                        evac_chunk(h, (kb - 4) // 4)
                evac_chunk(h, 3)
            else:
                for kb in range(NT):
                    emit_scores(h, kb)
                    emit_av(h, kb)
                for c in range(4):
                    evac_chunk(h, c)
            last_rd[h % 2] = last_sc[h]

        bb.set_guard()

        if _DBG == "qsp":
            toks = []
            for t in range(NT):
                toks.append(bb.op("gpsimd", lambda e, t=t: e.dma_start(
                    out=out3[t][0:69, :], in_=qsp[t].bitcast(F32)),
                    extra=tuple(bb.guard), dma=True))
            bb.op("gpsimd", lambda e: e.nop(), extra=tuple(toks))
            return "dbg"

        if _DBG == "avs":
            avf = avs.bitcast(F32)
            toks = []
            for t in range(8):
                toks.append(bb.op("gpsimd", lambda e, t=t: e.dma_start(
                    out=out3[t], in_=avf[:, t * 1024:(t + 1) * 1024]),
                    reads=(bavs,), dma=True))
            bb.op("gpsimd", lambda e: e.nop(), extra=tuple(toks))
            return "dbg"

        outb_s = ar.f32(1024)
        boutb = Buf()
        o2s = ar.f32(1024)
        bo2 = Buf()
        tt_sb = ar.f32(1024)
        btt = Buf()
        bD = [Buf(), Buf()]
        ld("sync", outb_s, outbr[:], boutb)
        for qt in range(NT):
            p = qt % 2
            sl = SL[p]
            xa, xbuf = get_x(qt, "e")
            psX = psA if p == 0 else psB
            fns = []
            for pr in range(8):
                for hh in range(2):
                    def f(e, pr=pr, hh=hh, qt=qt, psX=psX):
                        return e.matmul(psX[:, hh * 512:(hh + 1) * 512],
                                        avs[:, pr * S + qt * 128:pr * S + (qt + 1) * 128],
                                        woT[:, pr * 1024 + hh * 512:
                                            pr * 1024 + (hh + 1) * 512],
                                        start=(pr == 0), stop=(pr == 7))
                    fns.append(f)
            bb.group("tensor", fns, reads=(bavs, bwoT), writes=(bD[p],))
            v_op(lambda e, psX=psX: e.tensor_add(o2s, psX, outb_s),
                 reads=(bD[p], boutb), writes=(bo2,))
            stats(sl, o2s, bo2)
            rsqrt_chain(sl, sl.sg, sl.bsg, sl.mv[:, 1:2], sl.bmv, 1.0, 0)
            v_op(lambda e, sl=sl: e.tensor_scalar(sl.ngm, sl.mv[:, 0:1], sl.sg, -1.0,
                                                  op0=ALU.mult, op1=ALU.mult),
                 reads=(sl.bmv, sl.bsg), writes=(sl.bngm,))
            s_op(lambda e, sl=sl: e.activation(out=tt_sb, in_=o2s, func=ACT.Identity,
                                               scale=sl.sg, bias=sl.ngm),
                 reads=(bo2, sl.bsg, sl.bngm), writes=(btt,))
            tok = v_op(lambda e, xa=xa: e.tensor_add(xa, xa, tt_sb),
                       reads=(btt,), writes=(xbuf,))
            ln_precompute("f1", qt, xa, xbuf, sl,
                          var_scale=1.0 + 1e-3, eps_col=1)
            put_x(qt, tok, xa, xbuf)
        ar.release(m0)

    def get_x_l0(t, phase):
        return x_t[t], x_b[t]

    def put_x_l0(t, tok, xa, xbuf):
        pass

    for t in range(NT):
        ln_precompute("l0", t, x_t[t], x_b[t], SL[t % 2])
    attention(0, get_x_l0, put_x_l0, "l0", "f0")

    if _DBG == "l0a":
        toks = []
        for t in range(NT):
            toks.append(bb.op("gpsimd", lambda e, t=t: e.dma_start(out=out3[t], in_=x_t[t]),
                              reads=(x_b[t],), dma=True))
        bb.op("gpsimd", lambda e: e.nop(), extra=(toks[-1],))
        with nc.Block() as block:
            _realize(block)
        st.close()
        return nc

    for t in range(NT):
        bb.op("gpsimd", lambda e, t=t: e.dma_start(out=xsp3[t], in_=x_t[t]),
              reads=(x_b[t],), dma=True)
    bb.set_guard()
    ar.release(m_x)

    xw = [ar.f32(1024), ar.f32(1024)]
    xwb = [Buf(), Buf()]

    def xw_load(t, src3=xsp3):
        ld("sync", xw[t % 2], src3[t], xwb[t % 2])

    m_ffn0 = ar.mark()
    w1qT = ar.b16(DC * FF)
    bw1 = Buf()
    w2qT = ar.b16(32 * D)
    bw2 = Buf()
    b1_s = ar.f32(32)
    bb1 = Buf()
    b2_s = ar.f32(1024)
    bb2 = Buf()
    ld("sync", b1_s, b1l[:], bb1)
    ld("sync", b2_s, b2r[:], bb2)

    m_qscr = ar.mark()
    wstg = [ar.f32(2048), ar.f32(2048)]
    bwstg = [Buf(), Buf()]
    qa = ar.f32(2048)
    bqa = Buf()
    qb = ar.f32(2048)
    bqb = Buf()
    gp = ar.f32(48)
    bgp = Buf()
    wqn16 = ar.b16(2048)
    bwqn = Buf()
    W1r = W1h.rearrange("(a p) d -> p a d", p=128)
    W2r = W2h.rearrange("(a p) f -> a p f", p=128)

    for g in range(16):
        sl = wstg[g % 2]
        ld("gpsimd" if g % 2 == 0 else "sync",
           sl.rearrange("p (a d) -> p a d", d=1024),
           W1r[:, 2 * g:2 * g + 2, :], bwstg[g % 2])
        s_op(lambda e, sl=sl, g=g: e.activation(out=qa, in_=sl, func=ACT.Abs,
                                                accum_out=gp[:, g:g + 1]),
             reads=(bwstg[g % 2],), writes=(bqa, bgp))
    for g in range(16):
        sl = wstg[g % 2]
        ld("gpsimd" if g % 2 == 0 else "sync",
           sl, W2r[g // 2][:, (g % 2) * 2048:(g % 2 + 1) * 2048],
           bwstg[g % 2])
        s_op(lambda e, sl=sl, g=g: e.activation(out=qa, in_=sl, func=ACT.Abs,
                                                accum_out=gp[:, 16 + g:17 + g]),
             reads=(bwstg[g % 2],), writes=(bqa, bgp))
    for wi in range(2):
        cols = gp[:, 0:16] if wi == 0 else gp[:, 16:32]
        v_op(lambda e, cols=cols: e.reduce_sum(SL[0].tmp, cols, axis=AX),
             reads=(bgp,), writes=(SL[0].btmp,))
        t_op(lambda e: e.transpose(psmF[0:1, 0:128], SL[0].tmp, identf),
             reads=(SL[0].btmp, bidf), writes=(bM,))
        v_op(lambda e: e.reduce_sum(v_mk[0:1, 0:1], psmF[0:1, 0:128], axis=AX),
             reads=(bM,), writes=(bmk,))
        v_op(lambda e: e.tensor_scalar(v_mk[0:1, 1:2], v_mk[0:1, 0:1],
                                       1.0 / (FF * D), 1e-5, op0=ALU.mult, op1=ALU.add),
             reads=(bmk,), writes=(bmk,))
        v_op(lambda e: e.reciprocal(v_mk[0:1, 2:3], v_mk[0:1, 1:2]),
             reads=(bmk,), writes=(bmk,))
        t_op(lambda e: e.matmul(psmF[:, 0:1], ones_s, v_mk[0:1, 2:3],
                                start=True, stop=True),
             reads=(bmk, bone), writes=(bM,))
        v_op(lambda e, wi=wi: e.tensor_copy(gp[:, 40 + wi:41 + wi], psmF[:, 0:1]),
             reads=(bM,), writes=(bgp,))

    def quant(src, bsrc, n, rcol):
        v_op(lambda e, src=src, n=n, rcol=rcol: e.tensor_scalar(
            qa[:, 0:n], src, rcol, 0.5, op0=ALU.mult, op1=ALU.is_gt),
            reads=(bsrc, bgp), writes=(bqa,))
        v_op(lambda e, src=src, n=n, rcol=rcol: e.tensor_scalar(
            qb[:, 0:n], src, rcol, -0.5, op0=ALU.mult, op1=ALU.is_lt),
            reads=(bsrc, bgp), writes=(bqb,))
        v_op(lambda e, n=n: e.tensor_sub(wqn16[:, 0:n], qa[:, 0:n], qb[:, 0:n]),
             reads=(bqa, bqb), writes=(bwqn,))

    for c in range(32):
        sl = wstg[c % 2][:, 0:1024]
        ld("gpsimd" if c % 2 == 0 else "sync", sl, W1r[:, c, :], bwstg[c % 2])
        quant(sl, bwstg[c % 2], 1024, gp[:, 40:41])
        pb, pbb = (ptrB, bT) if c % 2 == 0 else (psmB, bM)
        fns = []
        for j in range(DC):
            def f(e, j=j, pb=pb):
                return e.transpose(pb[:, j * 128:(j + 1) * 128],
                                   wqn16[:, j * 128:(j + 1) * 128], identb)
            fns.append(f)
        bb.group("tensor", fns, reads=(bwqn, bid), writes=(pbb,))
        dsel = w1qT.rearrange("p (a f) -> p a f", f=FF)[:, :, c * 128:(c + 1) * 128]
        v_op(lambda e, dsel=dsel, pb=pb: e.tensor_copy(
            dsel, pb[:, 0:1024].rearrange("p (a c) -> p a c", c=128)),
            reads=(pbb,), writes=(bw1,))
    for ch in range(16):
        dc, hf = ch // 2, ch % 2
        sl = wstg[ch % 2]
        ld("gpsimd" if ch % 2 == 0 else "sync",
           sl, W2r[dc][:, hf * 2048:(hf + 1) * 2048], bwstg[ch % 2])
        quant(sl, bwstg[ch % 2], 2048, gp[:, 41:42])
        for g in range(2):
            pb, pbb = (ptrB, bT) if g % 2 == 0 else (psmB, bM)
            fns = []
            for i in range(8):
                fc = hf * 16 + g * 8 + i
                def f(e, i2=g * 8 + i, i=i, pb=pb):
                    return e.transpose(pb[:, i * 128:(i + 1) * 128],
                                       wqn16[:, i2 * 128:(i2 + 1) * 128], identb)
                fns.append(f)
            bb.group("tensor", fns, reads=(bwqn, bid), writes=(pbb,))
            f0 = hf * 16 + g * 8
            dsel = w2qT[:, f0 * D:(f0 + 8) * D].rearrange(
                "p (a d) -> p a d", d=D)[:, :, dc * 128:(dc + 1) * 128]
            v_op(lambda e, dsel=dsel, pb=pb: e.tensor_copy(
                dsel, pb[:, 0:1024].rearrange("p (a c) -> p a c", c=128)),
                reads=(pbb,), writes=(bw2,))
    ar.release(m_qscr)
    bb.set_guard()

    xnfm_sp = ar.b16(DC * 512)
    bxnf = Buf()
    xe = [ar.f32(1024), ar.f32(1024)]
    xeb = [Buf(), Buf()]
    h_sp = ar.b16(32 * 512)
    bh = Buf()
    xn16f = ar.b16(D)
    bxn0 = Buf()
    o2f = ar.f32(1024)
    bo2f = Buf()
    ttf = ar.f32(1024)
    bttf = Buf()

    xw_load(0)
    for sp in range(4):
        for tl in range(4):
            tc = sp * 4 + tl
            if tc + 1 < NT:
                xw_load(tc + 1)
            xa, xbuf = xw[tc % 2], xwb[tc % 2]
            ln_apply_pre("f0", tc, xn16f, bxn0, xa, xbuf)
            pb, pbb = (ptrB, bT) if tl % 2 == 0 else (psmB, bM)
            fns = []
            for j in range(DC):
                def f(e, j=j, pb=pb):
                    return e.transpose(pb[:, j * 128:(j + 1) * 128],
                                       xn16f[:, j * 128:(j + 1) * 128], identb)
                fns.append(f)
            bb.group("tensor", fns, reads=(bxn0, bid), writes=(pbb,))
            dsel = xnfm_sp.rearrange("p (a t) -> p a t", t=512)[:, :, tl * 128:(tl + 1) * 128]
            v_op(lambda e, dsel=dsel, pb=pb: e.tensor_copy(
                dsel, pb[:, 0:1024].rearrange("p (a c) -> p a c", c=128)),
                reads=(pbb,), writes=(bxnf,))
        for fc in range(32):
            ps, bps = ((psA[:, 0:512], bA), (psA[:, 512:1024], bA),
                       (psB[:, 0:512], bB), (psB[:, 512:1024], bB))[fc % 4]
            fns = []
            for j in range(DC):
                def f(e, j=j, fc=fc, ps=ps):
                    return e.matmul(ps, w1qT[:, j * FF + fc * 128:j * FF + (fc + 1) * 128],
                                    xnfm_sp[:, j * 512:(j + 1) * 512],
                                    start=(j == 0), stop=(j == DC - 1))
                fns.append(f)
            bb.group("tensor", fns, reads=(bw1, bxnf), writes=(bps,))
            s_op(lambda e, fc=fc, ps=ps: e.activation(
                out=h_sp[:, fc * 512:(fc + 1) * 512], in_=ps, func=ACT.Relu,
                scale=1.0, bias=b1_s[:, fc:fc + 1]),
                reads=(bps, bb1), writes=(bh,))
        for tl in range(4):
            tc = sp * 4 + tl
            ld("sync", xe[tc % 2], xsp3[tc], xeb[tc % 2])
            fns = []
            for fc in range(32):
                for hh in range(2):
                    def f(e, fc=fc, hh=hh, tl=tl):
                        return e.matmul(
                            psb[:, hh * 512:(hh + 1) * 512],
                            h_sp[:, fc * 512 + tl * 128:fc * 512 + (tl + 1) * 128],
                            w2qT[:, fc * D + hh * 512:fc * D + (hh + 1) * 512],
                            start=(fc == 0), stop=(fc == 31))
                    fns.append(f)
            bb.group("tensor", fns, reads=(bh, bw2), writes=(bG,))
            v_op(lambda e: e.tensor_add(o2f, psb, b2_s), reads=(bG, bb2), writes=(bo2f,))
            stats(SL[tl % 2], o2f, bo2f)
            ln_apply(SL[tl % 2], ttf, bttf, o2f, bo2f)
            xa, xbuf = xe[tc % 2], xeb[tc % 2]
            v_op(lambda e, xa=xa: e.tensor_add(xa, xa, ttf),
                 reads=(bttf,), writes=(xbuf,))
            ln_precompute("l1", tc, xa, xbuf, SL[tl % 2])
            bb.op("gpsimd", lambda e, tc=tc, xa=xa: e.dma_start(out=xsp3[tc], in_=xa),
                  reads=(xbuf,), dma=True)
            if _DBG == "ffn0":
                bb.op("gpsimd", lambda e, tc=tc, xa=xa: e.dma_start(out=out3[tc], in_=xa),
                      reads=(xbuf,), dma=True)
    if _DBG == "ffn0":
        bb.op("gpsimd", lambda e: e.nop(), extra=(("dma_gpsimd", sch.ndma["gpsimd"]),))
        with nc.Block() as block:
            _realize(block)
        st.close()
        return nc
    bb.set_guard()
    ar.release(m_ffn0)

    _xl1 = {"kv": -1, "q": -1, "e": -1}

    def get_x_l1(t, phase):
        if _xl1[phase] < t:
            xw_load(t)
            _xl1[phase] = t
        if t + 1 < NT and _xl1[phase] < t + 1:
            xw_load(t + 1)
            _xl1[phase] = t + 1
        return xw[t % 2], xwb[t % 2]

    def put_x_l1(t, tok, xa, xbuf):
        bb.op("gpsimd", lambda e, t=t, xa=xa: e.dma_start(out=xsp3[t], in_=xa),
              reads=(xbuf,), dma=True)

    _r1 = attention1_st(get_x_l1, put_x_l1)
    if _r1 == "dbg":
        with nc.Block() as block:
            _realize(block)
        st.close()
        return nc

    if _DBG == "l1a":
        toks = []
        xw_load(0)
        for t in range(NT):
            if t + 1 < NT:
                xw_load(t + 1)
            toks.append(bb.op("gpsimd", lambda e, t=t: e.dma_start(
                out=out3[t], in_=xw[t % 2]), reads=(xwb[t % 2],), dma=True))
        bb.op("gpsimd", lambda e: e.nop(), extra=(toks[-1],))
        with nc.Block() as block:
            _realize(block)
        st.close()
        return nc
    bb.set_guard()

    m_f1 = ar.mark()
    xn_all = [ar.f32(1024) for _ in range(NT)]
    bxna = [Buf() for _ in range(NT)]
    wnat = ar.f32(DC * 1024)
    bwn = Buf()
    wq16 = ar.b16(DC * 1024)
    bwq = Buf()
    gam_s = ar.f32(1024)
    bgam = Buf()
    bet_s = ar.f32(1024)
    bbet = Buf()
    f1a = ar.f32(1024)
    bf1a = Buf()
    f1b = ar.f32(1024)
    bf1b = Buf()
    xqfm = ar.b16(DC * 128)
    bxqf = Buf()
    xq16 = ar.b16(D)
    bxq16 = Buf()
    mx_r = v_mk[:, 4:5]

    ld("sync", wnat.rearrange("p (a u) -> p a u", u=1024),
       wf1.rearrange("(a p) u -> p a u", p=128), bwn)
    ld("sync", gam_s, gamr[:], bgam)
    ld("sync", bet_s, betr[:], bbet)

    for j in range(DC):
        s_op(lambda e, j=j: e.activation(out=f1a, in_=wnat[:, j * 1024:(j + 1) * 1024],
                                         func=ACT.Abs, accum_out=kn_t[:, j:j + 1]),
             reads=(bwn,), writes=(bf1a, bknt))
    v_op(lambda e: e.reduce_sum(SL[0].tmp, kn_t[:, 0:8], axis=AX), reads=(bknt,), writes=(SL[0].btmp,))
    t_op(lambda e: e.transpose(psmF[0:1, 0:128], SL[0].tmp, identf), reads=(SL[0].btmp, bidf), writes=(bM,))
    v_op(lambda e: e.reduce_sum(v_mk[0:1, 0:1], psmF[0:1, 0:128], axis=AX),
         reads=(bM,), writes=(bmk,))
    v_op(lambda e: e.tensor_scalar(v_mk[0:1, 1:2], v_mk[0:1, 0:1], 1.0 / (D * D), 1e-5,
                                   op0=ALU.mult, op1=ALU.add),
         reads=(bmk,), writes=(bmk,))
    v_op(lambda e: e.reciprocal(v_mk[0:1, 2:3], v_mk[0:1, 1:2]), reads=(bmk,), writes=(bmk,))
    t_op(lambda e: e.matmul(psmF[:, 0:1], ones_s, v_mk[0:1, 2:3], start=True, stop=True),
         reads=(bmk, bone), writes=(bM,))
    v_op(lambda e: e.tensor_copy(pv_s[:, 2:3], psmF[:, 0:1]), reads=(bM,), writes=(bpv,))
    for j in range(DC):
        wsl = wnat[:, j * 1024:(j + 1) * 1024]
        v_op(lambda e, wsl=wsl: e.tensor_scalar(f1a, wsl, pv_s[:, 2:3], 0.5,
                                                op0=ALU.mult, op1=ALU.is_gt),
             reads=(bwn, bpv), writes=(bf1a,))
        v_op(lambda e, wsl=wsl: e.tensor_scalar(f1b, wsl, pv_s[:, 2:3], -0.5,
                                                op0=ALU.mult, op1=ALU.is_lt),
             reads=(bwn, bpv), writes=(bf1b,))
        v_op(lambda e, j=j: e.tensor_sub(wq16[:, j * 1024:(j + 1) * 1024], f1a, f1b),
             reads=(bf1a, bf1b), writes=(bwq,))

    xw_load(0)
    for tc in range(NT):
        if tc + 1 < NT:
            xw_load(tc + 1)
        xa, xbuf = xw[tc % 2], xwb[tc % 2]
        ln_apply_pre("f1", tc, f1a, bf1a, xa, xbuf)
        v_op(lambda e, tc=tc: e.tensor_mul(xn_all[tc], f1a, gam_s),
             reads=(bf1a, bgam), writes=(bxna[tc],))
        v_op(lambda e, tc=tc: e.tensor_add(xn_all[tc], xn_all[tc], bet_s),
             reads=(bbet,), writes=(bxna[tc],), extra=tuple(bxna[tc].w))
        v_op(lambda e, tc=tc: e.tensor_reduce(kn_t[:, 15:16], xn_all[tc], axis=AX,
                                              op=ALU.max, apply_absolute_value=True),
             reads=(bxna[tc],), writes=(bknt,))
        if tc == 0:
            v_op(lambda e: e.tensor_copy(mx_r, kn_t[:, 15:16]), reads=(bknt,), writes=(bmk,))
        else:
            v_op(lambda e: e.tensor_max(mx_r, mx_r, kn_t[:, 15:16]),
                 reads=(bknt,), writes=(bmk,), extra=tuple(bmk.w))
    t_op(lambda e: e.transpose(psmF[0:1, 0:128], mx_r, identf), reads=(bmk, bidf), writes=(bM,))
    v_op(lambda e: e.reduce_max(v_mk[0:1, 6:7], psmF[0:1, 0:128], axis=AX),
         reads=(bM,), writes=(bmk,))
    bcc = Buf()
    bb.op("gpsimd", lambda e: e.dma_start(out=ccin[:], in_=v_mk[0:1, 6:7]),
          reads=(bmk,), dma=True)
    bb.op("gpsimd", lambda e: e.collective_compute(
        "AllReduce", ALU.max, replica_groups=[[0, 1, 2, 3, 4, 5, 6, 7]],
        ins=[ccin[:]], outs=[ccout[:]]),
        reads=(), writes=(bcc,), extra=tuple(sch.dma_all_tokens("gpsimd")))
    bb.op("gpsimd", lambda e: e.dma_start(out=v_mk[0:1, 7:8], in_=ccout[:]),
          reads=(bcc,), writes=(bmk,), dma=True)
    v_op(lambda e: e.reciprocal(v_mk[0:1, 8:9], v_mk[0:1, 7:8]), reads=(bmk,), writes=(bmk,))
    v_op(lambda e: e.tensor_scalar(v_mk[0:1, 9:10], v_mk[0:1, 8:9], 128.0, None,
                                   op0=ALU.mult),
         reads=(bmk,), writes=(bmk,))
    v_op(lambda e: e.tensor_scalar(v_mk[0:1, 10:11], v_mk[0:1, 1:2], 1.0, -1e-5,
                                   op0=ALU.mult, op1=ALU.add),
         reads=(bmk,), writes=(bmk,))
    v_op(lambda e: e.tensor_mul(v_mk[0:1, 11:12], v_mk[0:1, 10:11], v_mk[0:1, 7:8]),
         reads=(bmk,), writes=(bmk,), extra=tuple(bmk.w))
    v_op(lambda e: e.tensor_scalar(v_mk[0:1, 12:13], v_mk[0:1, 11:12], 1.0 / 128.0,
                                   None, op0=ALU.mult),
         reads=(bmk,), writes=(bmk,), extra=tuple(bmk.w))
    t_op(lambda e: e.matmul(psmF[:, 0:2], ones_s, v_mk[0:1, 9:13:3], start=True, stop=True),
         reads=(bmk, bone), writes=(bM,))
    sv_c = pv_s[:, 0:1]
    osc_c = pv_s[:, 1:2]
    v_op(lambda e: e.tensor_copy(sv_c, psmF[:, 0:1]), reads=(bM,), writes=(bpv,))
    v_op(lambda e: e.tensor_copy(osc_c, psmF[:, 1:2]), reads=(bM,), writes=(bpv,),
         extra=tuple(bpv.w))

    CLIP = 128.0 - 1e-5
    xw_load(0)
    ost = []
    for tc in range(NT):
        if tc + 1 < NT:
            xw_load(tc + 1)
        v_op(lambda e, tc=tc: e.tensor_scalar(f1a, xn_all[tc], sv_c, CLIP,
                                              op0=ALU.mult, op1=ALU.min),
             reads=(bxna[tc], bpv), writes=(bf1a,))
        v_op(lambda e: e.tensor_scalar(xq16, f1a, -CLIP, None, op0=ALU.max),
             reads=(bf1a,), writes=(bxq16,))
        pb, pbb = (ptrB, bT) if tc % 2 == 0 else (psmB, bM)
        fns = []
        for j in range(DC):
            def f(e, j=j, pb=pb):
                return e.transpose(pb[:, j * 128:(j + 1) * 128],
                                   xq16[:, j * 128:(j + 1) * 128], identb)
            fns.append(f)
        bb.group("tensor", fns, reads=(bxq16, bid), writes=(pbb,))
        v_op(lambda e, pb=pb: e.tensor_copy(
            xqfm.rearrange("p (a c) -> p a c", c=128),
            pb[:, 0:1024].rearrange("p (a c) -> p a c", c=128)),
            reads=(pbb,), writes=(bxqf,))
        fns = []
        for j in range(DC):
            for hh in range(2):
                def f(e, j=j, hh=hh):
                    return e.matmul(psb[:, hh * 512:(hh + 1) * 512],
                                    xqfm[:, j * 128:(j + 1) * 128],
                                    wq16[:, j * 1024 + hh * 512:j * 1024 + (hh + 1) * 512],
                                    start=(j == 0), stop=(j == DC - 1))
                fns.append(f)
        bb.group("tensor", fns, reads=(bxqf, bwq), writes=(bG,))
        sl = SL[tc % 2]
        stats(sl, psb, bG)
        v_op(lambda e, sl=sl: e.tensor_mul(sl.l2, osc_c, osc_c), reads=(bpv,), writes=(sl.bl2,))
        v_op(lambda e, sl=sl: e.tensor_mul(sl.tmp, sl.mv[:, 1:2], sl.l2),
             reads=(sl.bmv, sl.bl2), writes=(sl.btmp,))
        s_op(lambda e, sl=sl: e.activation(out=sl.l2, in_=sl.tmp, func=ACT.Ln, scale=1.0,
                                           bias=epsv_s[:, 0:1]),
             reads=(sl.btmp, beps), writes=(sl.bl2,))
        s_op(lambda e, sl=sl: e.activation(out=sl.sg, in_=sl.l2, func=ACT.Exp, scale=-0.5),
             reads=(sl.bl2,), writes=(sl.bsg,))
        v_op(lambda e, sl=sl: e.tensor_mul(sl.sg, sl.sg, osc_c), reads=(bpv,), writes=(sl.bsg,),
             extra=tuple(sl.bsg.w))
        v_op(lambda e, sl=sl: e.tensor_scalar(sl.ngm, sl.mv[:, 0:1], sl.sg, -1.0,
                                              op0=ALU.mult, op1=ALU.mult),
             reads=(sl.bmv, sl.bsg), writes=(sl.bngm,))
        s_op(lambda e, sl=sl: e.activation(out=f1a, in_=psb, func=ACT.Identity,
                                           scale=sl.sg, bias=sl.ngm),
             reads=(bG, sl.bsg, sl.bngm), writes=(bf1a,))
        xa, xbuf = xw[tc % 2], xwb[tc % 2]
        v_op(lambda e, xa=xa: e.tensor_add(xa, xa, f1a), reads=(bf1a,), writes=(xbuf,))
        tok = bb.op("gpsimd", lambda e, tc=tc, xa=xa: e.dma_start(out=out3[tc], in_=xa),
                    reads=(xbuf,), dma=True)
        ost.append(tok)

    bb.op("gpsimd", lambda e: e.nop(), extra=tuple(ost))

    with nc.Block() as block:
        _realize(block)
    st.close()
    return nc


_PROG = None


def kernel(L0_attn_x, L0_attn_w_q, L0_attn_w_k, L0_attn_w_v,
           L0_ffn_W1, L0_ffn_b1, L0_ffn_W2, L0_ffn_b2,
           L1_attn_wq, L1_attn_wk, L1_attn_wv, L1_attn_out_w, L1_attn_out_b,
           L1_ffn_weight, L1_ffn_ln_gamma, L1_ffn_ln_beta):
    global _PROG
    c = host_consts()
    if _PROG is None or _PROG[0] != _DBG:
        _PROG = (_DBG, build_program())
    nc = _PROG[1]

    f32 = np.float32
    x = np.ascontiguousarray(np.asarray(L0_attn_x, f32))
    common = {
        "w0q": np.ascontiguousarray(np.asarray(L0_attn_w_q, f32)),
        "w0k": np.ascontiguousarray(np.asarray(L0_attn_w_k, f32)),
        "w0v": np.ascontiguousarray(np.asarray(L0_attn_w_v, f32)),
        "W1": np.ascontiguousarray(np.asarray(L0_ffn_W1, f32)),
        "b1l": np.ascontiguousarray(np.asarray(L0_ffn_b1, f32).reshape(32, 128).T),
        "W2": np.ascontiguousarray(np.asarray(L0_ffn_W2, f32)),
        "b2r": np.ascontiguousarray(np.broadcast_to(np.asarray(L0_ffn_b2, f32), (128, D))),
        "w1q": np.ascontiguousarray(np.asarray(L1_attn_wq, f32)),
        "w1k": np.ascontiguousarray(np.asarray(L1_attn_wk, f32)),
        "w1v": np.ascontiguousarray(np.asarray(L1_attn_wv, f32)),
        "w1o": np.ascontiguousarray(np.asarray(L1_attn_out_w, f32)),
        "outbr": np.ascontiguousarray(np.broadcast_to(np.asarray(L1_attn_out_b, f32), (128, D))),
        "wf1": np.ascontiguousarray(np.asarray(L1_ffn_weight, f32)),
        "gamr": np.ascontiguousarray(np.broadcast_to(np.asarray(L1_ffn_ln_gamma, f32), (128, D))),
        "betr": np.ascontiguousarray(np.broadcast_to(np.asarray(L1_ffn_ln_beta, f32), (128, D))),
        "identb": c["identb"], "identf": c["identf"], "maskt": c["mask"],
        "masktt": c["maskT"], "onesb": c["onesb"],
        "kaugt": c["kaug"], "qaugt": c["qaug"], "qslt": c["qslope"],
        "onesf": c["onesf"],
        "epsv": np.ascontiguousarray(
            np.broadcast_to(np.array([1e-5, 1e-8], np.float32), (128, 2))),
    }
    for k in ("c0k", "s0k", "c0q", "s0q", "c1k", "s1k", "c1q", "s1q"):
        common["t_" + k] = c[k]

    in_maps = []
    for ci in range(8):
        m = dict(common)
        m["x_in"] = np.ascontiguousarray(x[ci])
        in_maps.append(m)

    res = run_bass_kernel_spmd(nc, in_maps, core_ids=list(range(8)))
    outs = [np.asarray(res.results[ci]["out"], f32) for ci in range(8)]
    return np.stack(outs, axis=0)


# version 14 (final)
# speedup vs baseline: 1.5339x; 1.0099x over previous
import os as _os
import numpy as np
import ml_dtypes
from contextlib import ExitStack

_DBG = _os.environ.get("KDBG", "")

import concourse.bass as bass
import concourse.mybir as mybir
from concourse.bass_utils import run_bass_kernel_spmd

F32 = mybir.dt.float32
BF16 = mybir.dt.bfloat16
AX = mybir.AxisListType.X
ALU = mybir.AluOpType
ACT = mybir.ActivationFunctionType
BF = ml_dtypes.bfloat16

S = 2048
D = 1024
NT = S // 128
DC = D // 128
FF = 4096
NH = 16
HD = 64
NEG = -1.0e30
ARENA_N = 53208

ENGINES = ("tensor", "vector", "scalar", "sync", "gpsimd")

_CONST = {}


def host_consts():
    if _CONST:
        return _CONST
    pos = np.arange(S, dtype=np.float64)
    i = np.arange(D // 2, dtype=np.float64)
    theta0 = np.power(10000.0, -2.0 * (i - 1.0) / D)
    ang0 = pos[:, None] * theta0[None, :]
    freqs1 = 1.0 / np.power(10000.0, i / (D // 2))
    ang1 = pos[:, None] * freqs1[None, :]
    c = {}
    c["c0k"] = np.cos(ang0).astype(np.float32); c["s0k"] = np.sin(ang0).astype(np.float32)
    c["c0q"] = (np.cos(ang0) / 32.0).astype(np.float32); c["s0q"] = (np.sin(ang0) / 32.0).astype(np.float32)
    c["c1k"] = np.cos(ang1).astype(np.float32); c["s1k"] = np.sin(ang1).astype(np.float32)
    c["c1q"] = (np.cos(ang1) / 8.0).astype(np.float32); c["s1q"] = (np.sin(ang1) / 8.0).astype(np.float32)
    c["identb"] = np.eye(128, dtype=BF)
    c["identf"] = np.eye(128, dtype=np.float32)
    m = np.zeros((128, 128), np.float32)
    r = np.arange(128)
    m[r[:, None] < r[None, :]] = NEG
    c["mask"] = m
    c["maskT"] = np.ascontiguousarray(m.T)
    c["onesb"] = np.ones((128, 256), BF)
    slopes = np.array([2.0 ** (-0.5 * (h + 1)) for h in range(NH)], np.float64)
    kpos_hi = pos.astype(BF).astype(np.float64)
    kpos_lo = (pos - kpos_hi).astype(BF).astype(np.float64)
    c["kaug"] = np.ascontiguousarray(
        np.stack([np.ones(S, np.float64),
                  kpos_hi, kpos_lo, kpos_hi, kpos_lo]).astype(BF))
    s_hi = slopes.astype(BF).astype(np.float64)
    s_lo = (slopes - s_hi).astype(BF)
    qaug = np.zeros((5, NH * 128), BF)
    qaug[0, :] = BF(1.0)
    for h in range(NH):
        qaug[1, h * 128:(h + 1) * 128] = BF(s_hi[h])
        qaug[2, h * 128:(h + 1) * 128] = BF(s_hi[h])
        qaug[3, h * 128:(h + 1) * 128] = s_lo[h]
        qaug[4, h * 128:(h + 1) * 128] = s_lo[h]
    c["qaug"] = qaug
    qs = np.zeros((128, NT * NH), np.float32)
    for qt in range(NT):
        p = qt * 128 + np.arange(128, dtype=np.float64)
        for h in range(NH):
            qs[:, qt * NH + h] = (13.0 - slopes[h] * p).astype(np.float32)
    c["qslope"] = qs
    c["onesf"] = np.ones((1, 128), np.float32)
    _CONST.update(c)
    return c


DMA_POOL = {"sync": 14, "gpsimd": 8, "scalar": 2, "tensor": 1, "vector": 1}


class Sch:
    def __init__(self):
        self.items = {e: [] for e in ENGINES}
        self.ninc = {e: 0 for e in ENGINES}
        self.rr = {e: 0 for e in ENGINES}
        self.slot_cnt = {e: [0] * DMA_POOL[e] for e in ENGINES}
        self.slot_cons = {e: [None] * DMA_POOL[e] for e in ENGINES}

    def op(self, eng, fn, deps=(), inc=True):
        self.items[eng].append(("op", fn, deps, None, inc))
        if inc:
            self.ninc[eng] += 1
            return (eng, self.ninc[eng])
        return None

    def dma(self, eng, fn, deps=()):
        slot = self.rr[eng] % DMA_POOL[eng]
        self.rr[eng] += 1
        k = self.slot_cnt[eng][slot]
        deps = list(deps)
        if k > 0:
            c = self.slot_cons[eng][slot]
            deps.append(c if c is not None else (("dq", eng, slot), k))
        self.slot_cnt[eng][slot] = k + 1
        self.slot_cons[eng][slot] = None
        self.items[eng].append(("dma", fn, _prune(deps), slot, True))
        return (("dq", eng, slot), k + 1)

    def note_consumer(self, dep, tok):
        kind, n = dep
        if isinstance(kind, tuple) and kind[0] == "dq" and tok is not None:
            _, eng, slot = kind
            if n == self.slot_cnt[eng][slot] and self.slot_cons[eng][slot] is None:
                self.slot_cons[eng][slot] = tok

    def dma_all_tokens(self, eng):
        return [(("dq", eng, s), c) for s, c in enumerate(self.slot_cnt[eng]) if c > 0]


class Buf:
    __slots__ = ("w", "r", "fresh")
    def __init__(self):
        self.w = []
        self.r = []
        self.fresh = True


def _prune(deps):
    best = {}
    for k, n in deps:
        if n > best.get(k, -1):
            best[k] = n
    return tuple(best.items())


class B:
    def __init__(self, sch):
        self.sch = sch
        self.guard = ()

    def set_guard(self):
        g = [(e, self.sch.ninc[e]) for e in ENGINES if self.sch.ninc[e] > 0]
        for e in ENGINES:
            g += self.sch.dma_all_tokens(e)
        self.guard = tuple(g)

    def _deps(self, reads, writes, extra):
        deps = []
        for b in reads:
            deps += b.w
        for b in writes:
            deps += b.w + b.r
            if b.fresh:
                deps += list(self.guard)
        return _prune(deps + [d for d in extra if d is not None])

    def op(self, eng, fn, reads=(), writes=(), extra=(), dma=False):
        deps = self._deps(reads, writes, extra)
        tok = self.sch.dma(eng, fn, deps) if dma else self.sch.op(eng, fn, deps, True)
        if not dma:
            for d in deps:
                self.sch.note_consumer(d, tok)
        for b in reads:
            b.r.append(tok)
        for b in writes:
            b.w = [tok]
            b.r = []
            b.fresh = False
        return tok

    def group(self, eng, fns, reads=(), writes=(), extra=()):
        deps = self._deps(reads, writes, extra)
        tok = None
        for i, fn in enumerate(fns):
            t = self.sch.op(eng, fn, deps if i == 0 else (), i == len(fns) - 1)
            if t is not None:
                tok = t
        for d in deps:
            self.sch.note_consumer(d, tok)
        for b in reads:
            b.r.append(tok)
        for b in writes:
            b.w = [tok]
            b.r = []
            b.fresh = False
        return tok


class Arena:
    def __init__(self, ap, n):
        self.ap = ap
        self.n = n
        self.off = 0
        self.peak = 0

    def mark(self):
        return self.off

    def release(self, m):
        self.off = m

    def f32(self, n, parts=128):
        n = (n + 15) & ~15
        assert self.off + n <= self.n, f"arena overflow: {self.off}+{n}>{self.n}"
        ap = self.ap[0:parts, self.off:self.off + n]
        self.off += n
        self.peak = max(self.peak, self.off)
        return ap

    def b16(self, n, parts=128):
        nf = ((n + 1) // 2 + 15) & ~15
        base = self.f32(nf, 128)
        return base.bitcast(BF16)[0:parts, 0:n]


def build_program():
    nc = bass.Bass("TRN2", target_bir_lowering=False)

    def din(name, shape, dt=F32):
        return nc.dram_tensor(name, list(shape), dt, kind="ExternalInput").ap()

    x_in = din("x_in", (S, D))
    w0q = din("w0q", (D, D)); w0k = din("w0k", (D, D)); w0v = din("w0v", (D, D))
    W1h = din("W1", (FF, D)); b1l = din("b1l", (128, FF // 128))
    W2h = din("W2", (D, FF)); b2r = din("b2r", (128, D))
    w1q = din("w1q", (D, D)); w1k = din("w1k", (D, D)); w1v = din("w1v", (D, D))
    w1o = din("w1o", (D, D)); outbr = din("outbr", (128, D))
    wf1 = din("wf1", (D, D)); gamr = din("gamr", (128, D)); betr = din("betr", (128, D))
    tabs = {k: din("t_" + k, (S, D // 2)) for k in
            ("c0k", "s0k", "c0q", "s0q", "c1k", "s1k", "c1q", "s1q")}
    identb_h = din("identb", (128, 128), BF16)
    identf_h = din("identf", (128, 128))
    mask_h = din("maskt", (128, 128))
    maskT_h = din("masktt", (128, 128))
    onesb_h = din("onesb", (128, 256), BF16)
    kaug_h = din("kaugt", (5, S), BF16)
    qaug_h = din("qaugt", (5, NH * 128), BF16)
    qsl_h = din("qslt", (128, NT * NH))
    ones_h = din("onesf", (1, 128))
    epsv_h = din("epsv", (128, 2))

    out_h = nc.dram_tensor("out", [S, D], F32, kind="ExternalOutput").ap()
    xsp = nc.dram_tensor("xspill", [S, D], F32, kind="Internal").ap()
    qsp = nc.dram_tensor("qspill", [NT, 69, NH * 128], BF16, kind="Internal").ap()
    ccin = nc.dram_tensor("ccin", [1, 1], F32, kind="Internal").ap()
    ccout = nc.dram_tensor("ccout", [1, 1], F32, kind="Internal").ap()
    x3 = x_in.rearrange("(n p) d -> n p d", p=128)
    xsp3 = xsp.rearrange("(n p) d -> n p d", p=128)
    out3 = out_h.rearrange("(n p) d -> n p d", p=128)
    t3 = {k: v.rearrange("(n p) d -> n p d", p=128) for k, v in tabs.items()}

    sch = Sch()
    bb = B(sch)
    st = ExitStack()

    arena_t = st.enter_context(nc.sbuf_tensor("arena", [128, ARENA_N], F32))
    ar = Arena(arena_t.ap(), ARENA_N)

    psA_t = st.enter_context(nc.psum_tensor("psA", [128, 1024], F32))
    psB_t = st.enter_context(nc.psum_tensor("psB", [128, 1024], F32))
    psb_t = st.enter_context(nc.psum_tensor("psbig", [128, 1024], F32))
    ptr_t = st.enter_context(nc.psum_tensor("ptrb", [128, 512], F32))
    psm_t = st.enter_context(nc.psum_tensor("psmb", [128, 512], F32))
    psA = psA_t.ap(); psB = psB_t.ap(); psb = psb_t.ap()
    ptrF = ptr_t.ap(); psmF = psm_t.ap()
    ptrB = ptrF.bitcast(BF16); psmB = psmF.bitcast(BF16)
    bA, bB, bG, bT, bM = Buf(), Buf(), Buf(), Buf(), Buf()

    csem = {e: st.enter_context(nc.semaphore("s_" + e)) for e in ENGINES}
    dsem = {}
    for e in ENGINES:
        for sl_ in range(DMA_POOL[e]):
            if e in ("sync", "gpsimd", "scalar"):
                dsem[(e, sl_)] = st.enter_context(
                    nc.semaphore("d_%s_%d" % (e, sl_)))

    identb = ar.b16(128)
    identf = ar.f32(128)
    mask_s = ar.f32(128)
    maskT_s = ar.f32(128)
    qsl_s = ar.f32(NT * NH)
    ones_s = ar.f32(16)
    ones_s = ar.f32(128)[0:1, :]
    epsv_s = ar.f32(16)[:, 0:2]
    (bid, bidf, bmsk, bmskT, bqsl, bone, beps) = (Buf() for _ in range(7))

    class Slots:
        def __init__(self):
            a = ar.f32(40)
            self.st6 = a[:, 0:12]
            self.mv = a[:, 12:14]
            self.tmp = a[:, 14:15]
            self.sg = a[:, 15:16]
            self.ngm = a[:, 16:17]
            self.lc = a[:, 17:18]
            self.l2 = a[:, 18:19]
            self.lp = a[:, 20:24]
            self.beta = a[:, 24:40]
            (self.bst6, self.bmv, self.btmp, self.bsg, self.bngm, self.blc,
             self.bl2, self.blp, self.bbeta) = (Buf() for _ in range(9))

    SL = None
    v_mk = ar.f32(16)
    kn_run = ar.f32(16)
    kn_t = ar.f32(16)
    qn_t = ar.f32(16)
    mkb_s = ar.f32(16)
    pv_s = ar.f32(16)
    bmk, bknr, bknt, bqnt, bmkb, bpv = (Buf() for _ in range(6))

    def _realize(block):
        def mk_body(en):
            def body(e):
                waited = {}
                for kind, fn, deps, slot, inc in sch.items[en]:
                    for dk, dn in deps:
                        if isinstance(dk, tuple):
                            sem, tgt = dsem[(dk[1], dk[2])], 16 * dn
                        else:
                            sem, tgt = csem[dk], dn
                        if waited.get(dk, -1) >= tgt:
                            continue
                        e.wait_ge(sem, tgt)
                        waited[dk] = tgt
                    inst = fn(e)
                    if kind == "dma":
                        inst.then_inc(dsem[(en, slot)], 16)
                    elif inc:
                        inst.then_inc(csem[en], 1)
            return body
        block.tensor(mk_body("tensor"))
        block.vector(mk_body("vector"))
        block.scalar(mk_body("scalar"))
        block.sync(mk_body("sync"))
        block.gpsimd(mk_body("gpsimd"))

    def ld(eng, dst, src, buf, extra=()):
        return bb.op(eng, lambda e, d=dst, s=src: e.dma_start(out=d, in_=s),
                     writes=(buf,), extra=extra, dma=True)

    ld("sync", identb, identb_h[:], bid)
    ld("sync", identf, identf_h[:], bidf)
    ld("sync", mask_s, mask_h[:], bmsk)
    ld("sync", maskT_s, maskT_h[:], bmskT)
    ld("sync", qsl_s, qsl_h[:], bqsl)
    ld("sync", ones_s, ones_h[:], bone)
    ld("sync", epsv_s, epsv_h[:], beps)

    def v_op(fn, reads=(), writes=(), extra=()):
        return bb.op("vector", fn, reads, writes, extra)

    def s_op(fn, reads=(), writes=(), extra=()):
        return bb.op("scalar", fn, reads, writes, extra)

    def t_op(fn, reads=(), writes=(), extra=()):
        return bb.op("tensor", fn, reads, writes, extra)

    def rsqrt_chain(sl, dst, dstbuf, var_ap, varbuf, scale, eps_col):
        s_op(lambda e: e.activation(out=sl.tmp, in_=var_ap, func=ACT.Ln,
                                    scale=scale, bias=epsv_s[:, eps_col:eps_col + 1]),
             reads=(varbuf, beps), writes=(sl.btmp,))
        s_op(lambda e: e.activation(out=dst, in_=sl.tmp, func=ACT.Exp, scale=-0.5),
             reads=(sl.btmp,), writes=(dstbuf,))

    def stats(sl, src_ap, srcbuf):
        v_op(lambda e: e.bn_stats(sl.st6[:, 0:6], src_ap[:, 0:512]),
             reads=(srcbuf,), writes=(sl.bst6,))
        v_op(lambda e: e.bn_stats(sl.st6[:, 6:12], src_ap[:, 512:1024]),
             reads=(srcbuf,), writes=(sl.bst6,))
        v_op(lambda e: e.bn_aggr(sl.mv, sl.st6), reads=(sl.bst6,), writes=(sl.bmv,))

    def ln_apply(sl, dst_ap, dstbuf, src_ap, srcbuf, var_scale=1.0, eps_col=0):
        rsqrt_chain(sl, sl.sg, sl.bsg, sl.mv[:, 1:2], sl.bmv, var_scale, eps_col)
        v_op(lambda e: e.tensor_scalar(sl.ngm, sl.mv[:, 0:1], sl.sg, -1.0,
                                       op0=ALU.mult, op1=ALU.mult),
             reads=(sl.bmv, sl.bsg), writes=(sl.bngm,))
        s_op(lambda e: e.activation(out=dst_ap, in_=src_ap, func=ACT.Identity,
                                    scale=sl.sg, bias=sl.ngm),
             reads=(srcbuf, sl.bsg, sl.bngm), writes=(dstbuf,))

    SL = [Slots(), Slots()]

    class LnTab:
        def __init__(self):
            a = ar.f32(32)
            self.sg = a[:, 0:16]
            self.ngm = a[:, 16:32]
            self.b = Buf()
    _ltA, _ltB = LnTab(), LnTab()
    LT = {"l0": _ltA, "l1": _ltA, "f0": _ltB, "f1": _ltB}

    def ln_precompute(key, t, x_ap, xbuf, sl, var_scale=1.0, eps_col=0):
        lt = LT[key]
        stats(sl, x_ap, xbuf)
        rsqrt_chain(sl, lt.sg[:, t:t + 1], lt.b, sl.mv[:, 1:2], sl.bmv,
                    var_scale, eps_col)
        v_op(lambda e, lt=lt, t=t, sl=sl: e.tensor_scalar(
            lt.ngm[:, t:t + 1], sl.mv[:, 0:1], lt.sg[:, t:t + 1], -1.0,
            op0=ALU.mult, op1=ALU.mult),
            reads=(sl.bmv, lt.b), writes=(lt.b,))

    def ln_apply_pre(key, t, dst_ap, dstbuf, x_ap, xbuf):
        lt = LT[key]
        return s_op(lambda e, lt=lt, t=t: e.activation(
            out=dst_ap, in_=x_ap, func=ACT.Identity,
            scale=lt.sg[:, t:t + 1], bias=lt.ngm[:, t:t + 1]),
            reads=(xbuf, lt.b), writes=(dstbuf,))

    m_x = ar.mark()
    x_t = [ar.f32(1024) for _ in range(NT)]
    x_b = [Buf() for _ in range(NT)]
    for t in range(NT):
        ld("sync", x_t[t], x3[t], x_b[t])

    m_attn = ar.mark()

    def attention(layer, get_x, put_x, lnkey, lnout):
        L1 = layer == 1
        m0 = ar.mark()
        if L1:
            kaug = [ar.b16(S, parts=68) for _ in range(NH)]
            kaug_b = [Buf() for _ in range(NH)]
        else:
            kfm = ar.b16(DC * S)
            bkfm = Buf()
        vtm = ar.b16(NT * D)
        bvtm = Buf()
        wqT = ar.b16(DC * D)
        bwqT = Buf()
        if L1:
            woT = ar.b16(DC * D)
            bwoT = Buf()
        xn16 = [ar.b16(D), ar.b16(D)]
        bxn = [Buf(), Buf()]
        xnfm = [ar.b16(DC * 128), ar.b16(DC * 128)]
        bxnfm = [Buf(), Buf()]
        rsc0 = ar.f32(1024)
        rsc = [rsc0, rsc0]
        brsc0 = Buf()
        brsc = [brsc0, brsc0]
        rot16 = [ar.b16(D), ar.b16(D)]
        brot = [Buf(), Buf()]
        tabc = [ar.f32(512), ar.f32(512)]
        tabs_ = [ar.f32(512), ar.f32(512)]
        btc = [Buf(), Buf()]; bts = [Buf(), Buf()]
        m_kv = ar.mark()
        wkT = ar.b16(DC * D)
        bwkT = Buf()
        wvT = ar.b16(DC * D)
        bwvT = Buf()
        m_stage = ar.mark()
        wst = ar.b16(1024)
        bwst = Buf()

        ck, sk = ("c1k", "s1k") if L1 else ("c0k", "s0k")
        cq, sqn = ("c1q", "s1q") if L1 else ("c0q", "s0q")

        wlist = [(w1q, wqT, bwqT), (w1k, wkT, bwkT), (w1v, wvT, bwvT),
                 (w1o, woT, bwoT)] if L1 else \
                [(w0q, wqT, bwqT), (w0k, wkT, bwkT), (w0v, wvT, bwvT)]
        for whbm, wT, bwT in wlist:
            for hf in range(8):
                ld("gpsimd", wst, whbm.rearrange("(a p) d -> p a d", p=128)[:, hf, :],
                   bwst)
                for j in range(DC):
                    pb, pbb = (ptrB, bT) if j % 2 == 0 else (psmB, bM)
                    t_op(lambda e, j=j, pb=pb: e.transpose(
                        pb[:, 0:128], wst[:, j * 128:j * 128 + 128], identb),
                        reads=(bwst, bid), writes=(pbb,))
                    v_op(lambda e, j=j, hf=hf, pb=pb, wT=wT: e.tensor_copy(
                        wT[:, j * 1024 + hf * 128:j * 1024 + (hf + 1) * 128],
                        pb[:, 0:128]),
                        reads=(pbb,), writes=(bwT,))
        ar.release(m_stage)

        if L1:
            for h in range(NH):
                ld("sync", kaug[h][64:68, :], kaug_h[:], kaug_b[h])

        def ln_to_fm(sl, p, t, xa, xbuf):
            ln_apply_pre(lnkey, t, xn16[p], bxn[p], xa, xbuf)
            fns = []
            for j in range(DC):
                def f(e, j=j, p=p):
                    return e.transpose(ptrB[:, j * 128:(j + 1) * 128],
                                       xn16[p][:, j * 128:(j + 1) * 128], identb)
                fns.append(f)
            bb.group("tensor", fns, reads=(bxn[p], bid), writes=(bT,))
            v_op(lambda e, p=p: e.tensor_copy(
                xnfm[p].rearrange("p (a c) -> p a c", c=128),
                ptrB[:, 0:1024].rearrange("p (a c) -> p a c", c=128)),
                reads=(bT,), writes=(bxnfm[p],))

        def proj(p, wT, bwT, pd, bpd):
            fns = []
            for j in range(DC):
                for hh in range(2):
                    def f(e, j=j, hh=hh, wT=wT, pd=pd, p=p):
                        return e.matmul(pd[:, hh * 512:(hh + 1) * 512],
                                        xnfm[p][:, j * 128:(j + 1) * 128],
                                        wT[:, j * 1024 + hh * 512:
                                           j * 1024 + (hh + 1) * 512],
                                        start=(j == 0), stop=(j == DC - 1))
                    fns.append(f)
            bb.group("tensor", fns, reads=(bxnfm[p], bwT), writes=(bpd,))

        def rope(p, pd, bpd, ca, sa, bca, bsa):
            r0 = rsc[p][:, 0:512]; r1 = rsc[p][:, 512:1024]
            rb = brsc[p]; ro = brot[p]; rt = rot16[p]
            if L1:
                i0, i1 = pd[:, 0:512], pd[:, 512:1024]
                o0, o1 = rt[:, 0:512], rt[:, 512:1024]
                v_op(lambda e: e.tensor_mul(r0, i0, ca), reads=(bpd, bca), writes=(rb,))
                v_op(lambda e: e.tensor_mul(r1, i1, sa), reads=(bpd, bsa), writes=(rb,))
                v_op(lambda e: e.tensor_sub(o0, r0, r1), reads=(rb,), writes=(ro,))
                v_op(lambda e: e.tensor_mul(r0, i0, sa), reads=(bpd, bsa), writes=(rb,))
                v_op(lambda e: e.tensor_mul(r1, i1, ca), reads=(bpd, bca), writes=(rb,))
                v_op(lambda e: e.tensor_add(o1, r0, r1), reads=(rb,), writes=(ro,))
            else:
                te, to = pd[:, 0::2], pd[:, 1::2]
                oe, oo = rt[:, 0::2], rt[:, 1::2]
                v_op(lambda e: e.tensor_mul(r0, te, ca), reads=(bpd, bca), writes=(rb,))
                v_op(lambda e: e.tensor_mul(r1, to, sa), reads=(bpd, bsa), writes=(rb,))
                v_op(lambda e: e.tensor_add(oe, r0, r1), reads=(rb,), writes=(ro,))
                v_op(lambda e: e.tensor_mul(r0, to, ca), reads=(bpd, bca), writes=(rb,))
                v_op(lambda e: e.tensor_mul(r1, te, sa), reads=(bpd, bsa), writes=(rb,))
                v_op(lambda e: e.tensor_sub(oo, r0, r1), reads=(rb,), writes=(ro,))

        for tc in range(NT):
            p = tc % 2
            sl = SL[p]
            if tc == 0:
                ld("sync", tabc[0], t3[ck][0], btc[0])
                ld("sync", tabs_[0], t3[sk][0], bts[0])
            if tc + 1 < NT:
                ld("sync", tabc[(tc + 1) % 2], t3[ck][tc + 1], btc[(tc + 1) % 2])
                ld("sync", tabs_[(tc + 1) % 2], t3[sk][tc + 1], bts[(tc + 1) % 2])
            xa, xbuf = get_x(tc, "kv")
            ln_to_fm(sl, p, tc, xa, xbuf)
            proj(p, wkT, bwkT, psb, bG)
            proj(p, wvT, bwvT, psA, bA)
            rope(p, psb, bG, tabc[p], tabs_[p], btc[p], bts[p])
            if L1:
                v_op(lambda e, p=p: e.scalar_tensor_tensor(
                    rsc[p][:, 0:1024], rot16[p], 1.0, rot16[p],
                    op0=ALU.mult, op1=ALU.mult),
                    reads=(brot[p],), writes=(brsc[p],))
                v_op(lambda e, p=p: e.tensor_reduce(
                    kn_t, rsc[p][:, 0:1024].rearrange("p (h d) -> p h d", d=HD),
                    axis=AX, op=ALU.add),
                    reads=(brsc[p],), writes=(bknt,))
                if tc == 0:
                    v_op(lambda e: e.tensor_copy(kn_run, kn_t), reads=(bknt,), writes=(bknr,))
                else:
                    v_op(lambda e: e.tensor_max(kn_run, kn_run, kn_t),
                         reads=(bknt, bknr), writes=(bknr,))
            if L1:
                for h in range(NH - 1):
                    pb, pbb = (ptrB, bT) if h % 2 == 0 else (psmB, bM)
                    t_op(lambda e, h=h, pb=pb, p=p: e.transpose(
                        pb[:, 0:128], rot16[p][:, h * 64:h * 64 + 128], identb),
                        reads=(brot[p], bid), writes=(pbb,))
                    v_op(lambda e, h=h, pb=pb, tc=tc: e.tensor_copy(
                        kaug[h][0:64, tc * 128:(tc + 1) * 128], pb[0:64, 0:128]),
                        reads=(pbb,), writes=(kaug_b[h],))
                    if h == NH - 2:
                        v_op(lambda e, pb=pb, tc=tc: e.tensor_copy(
                            kaug[NH - 1][0:64, tc * 128:(tc + 1) * 128],
                            pb[64:128, 0:128]),
                            reads=(pbb,), writes=(kaug_b[NH - 1],))
            else:
                fns = []
                for j in range(DC):
                    def f(e, j=j, p=p):
                        return e.transpose(ptrB[:, j * 128:(j + 1) * 128],
                                           rot16[p][:, j * 128:(j + 1) * 128], identb)
                    fns.append(f)
                bb.group("tensor", fns, reads=(brot[p], bid), writes=(bT,))
                dsel = kfm.rearrange("p (a s) -> p a s", s=S)[:, :, tc * 128:(tc + 1) * 128]
                v_op(lambda e, dsel=dsel: e.tensor_copy(
                    dsel, ptrB[:, 0:1024].rearrange("p (a c) -> p a c", c=128)),
                    reads=(bT,), writes=(bkfm,))
            v_op(lambda e, tc=tc: e.tensor_copy(vtm[:, tc * D:(tc + 1) * D], psA),
                 reads=(bA,), writes=(bvtm,))

        if L1:
            t_op(lambda e: e.transpose(psmF[0:16, 0:128], kn_run, identf),
                 reads=(bknr, bidf), writes=(bM,))
            v_op(lambda e: e.reduce_max(v_mk[0:16, 0:1], psmF[0:16, 0:128], axis=AX),
                 reads=(bM,), writes=(bmk,))
            t_op(lambda e: e.transpose(ptrF[0:1, 0:16], v_mk[0:16, 0:1], identf[0:16, 0:16]),
                 reads=(bmk, bidf), writes=(bT,))
            v_op(lambda e: e.tensor_copy(v_mk[0:1, 0:16], ptrF[0:1, 0:16]),
                 reads=(bT,), writes=(bmk,))
            t_op(lambda e: e.matmul(psmF[:, 0:16], ones_s, v_mk[0:1, 0:16],
                                    start=True, stop=True),
                 reads=(bmk, bone), writes=(bM,))
            v_op(lambda e: e.tensor_copy(mkb_s, psmF[:, 0:16]), reads=(bM,), writes=(bmkb,))

        ar.release(m_kv)
        bb.set_guard()
        P_sb = [ar.b16(S), ar.b16(S)]
        bP = [Buf(), Buf()]
        PT_sb = ar.b16(S)
        bPT = Buf()
        tt_sb = ar.f32(1024)
        btt = Buf()
        if L1:
            qaugs = [ar.b16(NH * 128, parts=68), ar.b16(NH * 128, parts=68)]
            bqaug = [Buf(), Buf()]
            ld("sync", qaugs[0][64:68, :], qaug_h[:], bqaug[0])
            ld("sync", qaugs[1][64:68, :], qaug_h[:], bqaug[1])
            avs = ar.b16(8 * 128)
            bavs = Buf()
            o2s = ar.f32(1024)
            bo2 = Buf()
            outb_s = ar.f32(1024)
            boutb = Buf()
            ld("sync", outb_s, outbr[:], boutb)
        else:
            qfm = [ar.b16(DC * 128), ar.b16(DC * 128)]
            bqfm = [Buf(), Buf()]

        def stage_a(qt, midwork=None):
            p = qt % 2
            sl = SL[p]
            if qt == 0:
                ld("sync", tabc[0], t3[cq][0], btc[0])
                ld("sync", tabs_[0], t3[sqn][0], bts[0])
            if qt + 1 < NT:
                ld("sync", tabc[(qt + 1) % 2], t3[cq][qt + 1], btc[(qt + 1) % 2])
                ld("sync", tabs_[(qt + 1) % 2], t3[sqn][qt + 1], bts[(qt + 1) % 2])
            xa, xbuf = get_x(qt, "q")
            ln_to_fm(sl, p, qt, xa, xbuf)
            proj(p, wqT, bwqT, psb, bG)
            rope(p, psb, bG, tabc[p], tabs_[p], btc[p], bts[p])
            if midwork is not None:
                midwork()
            if L1:
                v_op(lambda e, p=p: e.scalar_tensor_tensor(
                    rsc[p][:, 0:1024], rot16[p], 1.0, rot16[p],
                    op0=ALU.mult, op1=ALU.mult),
                    reads=(brot[p],), writes=(brsc[p],))
                v_op(lambda e, p=p: e.tensor_reduce(
                    qn_t, rsc[p][:, 0:1024].rearrange("p (h d) -> p h d", d=HD),
                    axis=AX, op=ALU.add),
                    reads=(brsc[p],), writes=(bqnt,))
                v_op(lambda e, sl=sl: e.tensor_mul(sl.beta, qn_t, mkb_s),
                     reads=(bqnt, bmkb), writes=(sl.bbeta,))
                s_op(lambda e, sl=sl: e.activation(out=sl.beta, in_=sl.beta,
                                                   func=ACT.Ln, scale=1.0),
                     reads=(sl.bbeta,), writes=(sl.bbeta,))
                s_op(lambda e, sl=sl: e.activation(out=sl.beta, in_=sl.beta,
                                                   func=ACT.Exp, scale=0.5),
                     reads=(sl.bbeta,), writes=(sl.bbeta,))
                v_op(lambda e, qt=qt, sl=sl: e.scalar_tensor_tensor(
                    sl.beta, sl.beta, -1.0, qsl_s[:, qt * NH:(qt + 1) * NH],
                    op0=ALU.mult, op1=ALU.add),
                    reads=(sl.bbeta, bqsl), writes=(sl.bbeta,))
                for h in range(NH - 1):
                    pb, pbb = (ptrB, bT) if h % 2 == 0 else (psmB, bM)
                    t_op(lambda e, h=h, pb=pb, p=p: e.transpose(
                        pb[:, 0:128], rot16[p][:, h * 64:h * 64 + 128], identb),
                        reads=(brot[p], bid), writes=(pbb,))
                    v_op(lambda e, h=h, pb=pb, p=p: e.tensor_copy(
                        qaugs[p][0:64, h * 128:(h + 1) * 128], pb[0:64, 0:128]),
                        reads=(pbb,), writes=(bqaug[p],))
                    if h == NH - 2:
                        v_op(lambda e, pb=pb, p=p: e.tensor_copy(
                            qaugs[p][0:64, (NH - 1) * 128:NH * 128], pb[64:128, 0:128]),
                            reads=(pbb,), writes=(bqaug[p],))
            else:
                fns = []
                for j in range(DC):
                    def f(e, j=j, p=p):
                        return e.transpose(ptrB[:, j * 128:(j + 1) * 128],
                                           rot16[p][:, j * 128:(j + 1) * 128], identb)
                    fns.append(f)
                bb.group("tensor", fns, reads=(brot[p], bid), writes=(bT,))
                v_op(lambda e, p=p: e.tensor_copy(
                    qfm[p].rearrange("p (a c) -> p a c", c=128),
                    ptrB[:, 0:1024].rearrange("p (a c) -> p a c", c=128)),
                    reads=(bT,), writes=(bqfm[p],))
            return xa, xbuf

        def segs_of(span):
            out = [(0, min(span, 1024), psA, bA)]
            if span > 1024:
                out.append((1024, span, psB, bB))
            return out

        def flush_pt_av(qt, h, hp):
            span = (qt + 1) * 128
            nkb = qt + 1
            for r0 in range(0, nkb, 8):
                r1 = min(r0 + 8, nkb)
                pb, pbb = (ptrB, bT) if (r0 // 8) % 2 == 0 else (psmB, bM)
                fns = []
                for kb in range(r0, r1):
                    def f(e, kb=kb, r0=r0, pb=pb, hp=hp):
                        return e.transpose(
                            pb[:, (kb - r0) * 128:(kb - r0 + 1) * 128],
                            P_sb[hp][:, kb * 128:(kb + 1) * 128], identb)
                    fns.append(f)
                bb.group("tensor", fns, reads=(bP[hp], bid), writes=(pbb,))
                nb = r1 - r0
                v_op(lambda e, r0=r0, nb=nb, pb=pb: e.tensor_copy(
                    PT_sb[:, r0 * 128:(r0 + nb) * 128].rearrange(
                        "p (a c) -> p a c", c=128),
                    pb[:, 0:nb * 128].rearrange("p (a c) -> p a c", c=128)),
                    reads=(pbb,), writes=(bPT,))
            if L1:
                pr, po = h // 2, (h % 2) * 64
                fns = []
                for kb in range(nkb):
                    def f(e, kb=kb, h=h, pr=pr, po=po):
                        return e.matmul(
                            psb[po:po + 64, pr * 128:(pr + 1) * 128],
                            vtm[:, kb * D + h * HD:kb * D + (h + 1) * HD],
                            PT_sb[:, kb * 128:(kb + 1) * 128],
                            start=(kb == 0), stop=(kb == nkb - 1),
                            tile_position=(0, po))
                    fns.append(f)
                bb.group("tensor", fns, reads=(bvtm, bPT), writes=(bG,))
            else:
                fns = []
                for kb in range(nkb):
                    for hh in range(2):
                        def f(e, kb=kb, hh=hh):
                            return e.matmul(
                                psb[:, hh * 512:(hh + 1) * 512],
                                PT_sb[:, kb * 128:(kb + 1) * 128],
                                vtm[:, kb * D + hh * 512:kb * D + (hh + 1) * 512],
                                start=(kb == 0), stop=(kb == nkb - 1))
                        fns.append(f)
                bb.group("tensor", fns, reads=(bPT, bvtm), writes=(bG,))

        def l0_epilogue(qt, xa, xbuf, lcol_info):
            sl = SL[qt % 2]
            lc_ap, lc_buf = lcol_info
            stats(sl, psb, bG)
            v_op(lambda e, sl=sl: e.tensor_mul(sl.l2, lc_ap, lc_ap),
                 reads=(lc_buf,), writes=(sl.bl2,))
            v_op(lambda e, sl=sl: e.reciprocal(sl.l2, sl.l2),
                 reads=(sl.bl2,), writes=(sl.bl2,))
            v_op(lambda e, sl=sl: e.scalar_tensor_tensor(
                sl.tmp, sl.l2, 1e-5, sl.mv[:, 1:2], op0=ALU.mult, op1=ALU.add),
                reads=(sl.bl2, sl.bmv), writes=(sl.btmp,))
            s_op(lambda e, sl=sl: e.activation(out=sl.l2, in_=sl.tmp, func=ACT.Ln,
                                               scale=1.0),
                 reads=(sl.btmp,), writes=(sl.bl2,))
            s_op(lambda e, sl=sl: e.activation(out=sl.sg, in_=sl.l2, func=ACT.Exp,
                                               scale=-0.5),
                 reads=(sl.bl2,), writes=(sl.bsg,))
            v_op(lambda e, sl=sl: e.tensor_scalar(sl.ngm, sl.mv[:, 0:1], sl.sg, -1.0,
                                                  op0=ALU.mult, op1=ALU.mult),
                 reads=(sl.bmv, sl.bsg), writes=(sl.bngm,))
            s_op(lambda e, sl=sl: e.activation(out=tt_sb, in_=psb, func=ACT.Identity,
                                               scale=sl.sg, bias=sl.ngm),
                 reads=(bG, sl.bsg, sl.bngm), writes=(btt,))
            tok = v_op(lambda e, xa=xa: e.tensor_add(xa, xa, tt_sb),
                       reads=(btt,), writes=(xbuf,))
            if lnout is not None:
                ln_precompute(lnout, qt, xa, xbuf, sl)
            put_x(qt, tok, xa, xbuf)

        def l1_av_evac():
            v_op(lambda e: e.tensor_copy(avs, psb), reads=(bG,), writes=(bavs,))

        def l1_epilogue(qt, xa, xbuf):
            sl = SL[qt % 2]
            fns = []
            for pr in range(8):
                for hh in range(2):
                    def f(e, pr=pr, hh=hh):
                        return e.matmul(psA[:, hh * 512:(hh + 1) * 512],
                                        avs[:, pr * 128:(pr + 1) * 128],
                                        woT[:, pr * 1024 + hh * 512:
                                            pr * 1024 + (hh + 1) * 512],
                                        start=(pr == 0), stop=(pr == 7))
                    fns.append(f)
            bb.group("tensor", fns, reads=(bavs, bwoT), writes=(bA,))
            v_op(lambda e: e.tensor_add(o2s, psA, outb_s),
                 reads=(bA, boutb), writes=(bo2,))
            stats(sl, o2s, bo2)
            rsqrt_chain(sl, sl.sg, sl.bsg, sl.mv[:, 1:2], sl.bmv, 1.0, 0)
            v_op(lambda e, sl=sl: e.tensor_scalar(sl.ngm, sl.mv[:, 0:1], sl.sg, -1.0,
                                                  op0=ALU.mult, op1=ALU.mult),
                 reads=(sl.bmv, sl.bsg), writes=(sl.bngm,))
            s_op(lambda e, sl=sl: e.activation(out=tt_sb, in_=o2s, func=ACT.Identity,
                                               scale=sl.sg, bias=sl.ngm),
                 reads=(bo2, sl.bsg, sl.bngm), writes=(btt,))
            tok = v_op(lambda e, xa=xa: e.tensor_add(xa, xa, tt_sb),
                       reads=(btt,), writes=(xbuf,))
            if lnout is not None:
                ln_precompute(lnout, qt, xa, xbuf, sl,
                              var_scale=1.0 + 1e-3, eps_col=1)
            put_x(qt, tok, xa, xbuf)

        if L1:
            xe = [ar.f32(1024), ar.f32(1024)]
            xeb = [Buf(), Buf()]
            pend = [None]
            pend_ep = [None]

            def flush_seam():
                if pend_ep[0] is not None:
                    q_ = pend_ep[0]
                    ld("sync", xe[q_ % 2], xsp3[q_], xeb[q_ % 2])
                    l1_epilogue(q_, xe[q_ % 2], xeb[q_ % 2])
                    pend_ep[0] = None

            for qt in range(NT):
                p = qt % 2
                sl = SL[p]
                xa, xbuf = stage_a(qt, midwork=flush_seam)
                span = (qt + 1) * 128
                segs = segs_of(span)
                for h in range(NH):
                    hp = h % 2
                    use_segs = segs
                    if span <= 1024:
                        ps_, bps_ = (psA, bA) if hp == 0 else (psB, bB)
                        use_segs = [(0, span, ps_, bps_)]
                    for (lo, hi, ps, bps) in use_segs:
                        fns = []
                        for c0 in range(lo, hi, 512):
                            c1 = min(c0 + 512, hi)
                            def f(e, h=h, c0=c0, c1=c1, ps=ps, lo=lo, p=p):
                                return e.matmul(ps[:, c0 - lo:c1 - lo],
                                                qaugs[p][:, h * 128:(h + 1) * 128],
                                                kaug[h][:, c0:c1],
                                                start=True, stop=True)
                            fns.append(f)
                        bb.group("tensor", fns, reads=(bqaug[p], kaug_b[h]), writes=(bps,))
                    dlo = span - 128
                    mseg = use_segs[-1]
                    mlo = mseg[0]
                    msp, msb = mseg[2], mseg[3]
                    v_op(lambda e, dlo=dlo, msp=msp, mlo=mlo: e.tensor_add(
                        msp[:, dlo - mlo:dlo - mlo + 128],
                        msp[:, dlo - mlo:dlo - mlo + 128], mask_s),
                        reads=(bmsk,), writes=(msb,))
                    if pend[0] is not None:
                        q_, h_, hp_ = pend[0]
                        flush_pt_av(q_, h_, hp_)
                        pend[0] = None
                    bcol = sl.beta[:, h:h + 1]
                    hsl = SL[hp]
                    for gi, (lo, hi, ps, bps) in enumerate(use_segs):
                        s_op(lambda e, lo=lo, hi=hi, ps=ps, gi=gi, bcol=bcol, hp=hp,
                             hsl=hsl: e.activation(
                                 out=P_sb[hp][:, lo:hi], in_=ps[:, 0:hi - lo],
                                 func=ACT.Exp, scale=1.0, bias=bcol,
                                 accum_out=hsl.lp[:, gi:gi + 1]),
                             reads=(bps, sl.bbeta), writes=(bP[hp], hsl.blp))
                    if len(use_segs) == 1:
                        v_op(lambda e, hsl=hsl: e.reciprocal(hsl.lc, hsl.lp[:, 0:1]),
                             reads=(hsl.blp,), writes=(hsl.blc,))
                    else:
                        v_op(lambda e, hsl=hsl: e.tensor_add(hsl.lc, hsl.lp[:, 0:1],
                                                             hsl.lp[:, 1:2]),
                             reads=(hsl.blp,), writes=(hsl.blc,))
                        v_op(lambda e, hsl=hsl: e.reciprocal(hsl.lc, hsl.lc),
                             reads=(hsl.blc,), writes=(hsl.blc,))
                    v_op(lambda e, span=span, hp=hp, hsl=hsl: e.tensor_scalar(
                        P_sb[hp][:, 0:span], P_sb[hp][:, 0:span], hsl.lc, None,
                        op0=ALU.mult),
                        reads=(hsl.blc,), writes=(bP[hp],))
                    pend[0] = (qt, h, hp)
                q_, h_, hp_ = pend[0]
                flush_pt_av(q_, h_, hp_)
                pend[0] = None
                l1_av_evac()
                pend_ep[0] = qt
            flush_seam()
        else:
            pend = [None]

            def flush_seam0():
                if pend[0] is not None:
                    q_, xa_, xb_, lci = pend[0]
                    flush_pt_av(q_, 0, q_ % 2)
                    l0_epilogue(q_, xa_, xb_, lci)
                    pend[0] = None

            for qt in range(NT):
                p = qt % 2
                sl = SL[p]
                xa, xbuf = stage_a(qt, midwork=flush_seam0)
                span = (qt + 1) * 128
                segs = segs_of(span)
                for (lo, hi, ps, bps) in segs:
                    for c0 in range(lo, hi, 512):
                        c1 = min(c0 + 512, hi)
                        fns = []
                        for j in range(DC):
                            def f(e, j=j, c0=c0, c1=c1, ps=ps, lo=lo, p=p):
                                return e.matmul(ps[:, c0 - lo:c1 - lo],
                                                qfm[p][:, j * 128:(j + 1) * 128],
                                                kfm[:, j * S + c0:j * S + c1],
                                                start=(j == 0), stop=(j == DC - 1))
                            fns.append(f)
                        bb.group("tensor", fns, reads=(bqfm[p], bkfm), writes=(bps,))
                dlo = span - 128
                ps, bps, off = (psA, bA, 0) if dlo < 1024 else (psB, bB, 1024)
                v_op(lambda e, dlo=dlo, ps=ps, off=off: e.tensor_add(
                    ps[:, dlo - off:dlo - off + 128],
                    ps[:, dlo - off:dlo - off + 128], mask_s),
                    reads=(bmsk,), writes=(bps,))
                for gi, (lo, hi, ps2, bps2) in enumerate(segs):
                    v_op(lambda e, ps2=ps2, hi=hi, lo=lo, gi=gi, sl=sl: e.reduce_max(
                        sl.lp[:, 2 + gi:3 + gi], ps2[:, 0:hi - lo], axis=AX),
                        reads=(bps2,), writes=(sl.blp,))
                if len(segs) == 2:
                    v_op(lambda e, sl=sl: e.tensor_max(sl.lp[:, 2:3], sl.lp[:, 2:3],
                                                       sl.lp[:, 3:4]),
                         reads=(sl.blp,), writes=(sl.blp,))
                v_op(lambda e, sl=sl: e.tensor_scalar(sl.beta[:, 0:1], sl.lp[:, 2:3],
                                                      -1.0, None, op0=ALU.mult),
                     reads=(sl.blp,), writes=(sl.bbeta,))
                for gi, (lo, hi, ps, bps) in enumerate(segs):
                    s_op(lambda e, lo=lo, hi=hi, ps=ps, gi=gi, sl=sl, p=p:
                         e.activation(out=P_sb[p][:, lo:hi], in_=ps[:, 0:hi - lo],
                                      func=ACT.Exp, scale=1.0, bias=sl.beta[:, 0:1],
                                      accum_out=sl.lp[:, gi:gi + 1]),
                         reads=(bps, sl.bbeta), writes=(bP[p], sl.blp))
                if len(segs) == 1:
                    v_op(lambda e, sl=sl: e.reciprocal(sl.lc, sl.lp[:, 0:1]),
                         reads=(sl.blp,), writes=(sl.blc,))
                else:
                    v_op(lambda e, sl=sl: e.tensor_add(sl.lc, sl.lp[:, 0:1],
                                                       sl.lp[:, 1:2]),
                         reads=(sl.blp,), writes=(sl.blc,))
                    v_op(lambda e, sl=sl: e.reciprocal(sl.lc, sl.lc),
                         reads=(sl.blc,), writes=(sl.blc,))
                pend[0] = (qt, xa, xbuf, (sl.lc, sl.blc))
            flush_seam0()
        ar.release(m0)

    def attention1_st(get_x, put_x):
        m0 = ar.mark()
        kaug = [ar.b16(S, parts=69) for _ in range(NH)]
        kaug_b = [Buf() for _ in range(NH)]
        vtm = ar.b16(NT * NH * 65)
        bvtm = Buf()
        woT = ar.b16(DC * D)
        bwoT = Buf()
        betaT = ar.b16(S, parts=16)
        bbetaT = Buf()

        m_q0 = ar.mark()
        wqT = ar.b16(DC * D)
        bwqT = Buf()
        xn16 = [ar.b16(D), ar.b16(D)]
        bxn = [Buf(), Buf()]
        xnfm = [ar.b16(DC * 128), ar.b16(DC * 128)]
        bxnfm = [Buf(), Buf()]
        rsc0 = ar.f32(1024)
        rsc = [rsc0, rsc0]
        brsc0 = Buf()
        brsc = [brsc0, brsc0]
        rot16 = [ar.b16(D), ar.b16(D)]
        brot = [Buf(), Buf()]
        tabc = [ar.f32(512), ar.f32(512)]
        tabs_ = [ar.f32(512), ar.f32(512)]
        btc = [Buf(), Buf()]; bts = [Buf(), Buf()]
        m_kv = ar.mark()
        wkT = ar.b16(DC * D)
        bwkT = Buf()
        wvT = ar.b16(DC * D)
        bwvT = Buf()
        m_stage = ar.mark()
        wst = ar.b16(1024)
        bwst = Buf()

        for whbm, wT, bwT in ((w1q, wqT, bwqT), (w1k, wkT, bwkT),
                              (w1v, wvT, bwvT), (w1o, woT, bwoT)):
            for hf in range(8):
                ld("gpsimd", wst, whbm.rearrange("(a p) d -> p a d", p=128)[:, hf, :],
                   bwst)
                for j in range(DC):
                    pb, pbb = (ptrB, bT) if j % 2 == 0 else (psmB, bM)
                    t_op(lambda e, j=j, pb=pb: e.transpose(
                        pb[:, 0:128], wst[:, j * 128:j * 128 + 128], identb),
                        reads=(bwst, bid), writes=(pbb,))
                    v_op(lambda e, j=j, hf=hf, pb=pb, wT=wT: e.tensor_copy(
                        wT[:, j * 1024 + hf * 128:j * 1024 + (hf + 1) * 128],
                        pb[:, 0:128]),
                        reads=(pbb,), writes=(bwT,))
        for h in range(NH):
            ld("sync", kaug[h][64:69, :], kaug_h[:], kaug_b[h])
        ld("sync", wst[:, 0:256], onesb_h[:], bwst)
        vtm4 = vtm.rearrange("p (t h c) -> p t h c", t=NT, h=NH)
        v_op(lambda e: e.tensor_copy(
            vtm4[:, :, :, 64:65],
            wst[:, 0:256].rearrange("p (t h c) -> p t h c", t=NT, c=1)),
            reads=(bwst,), writes=(bvtm,))
        ar.release(m_stage)

        def ln_to_fm(p, t, xa, xbuf):
            ln_apply_pre("l1", t, xn16[p], bxn[p], xa, xbuf)
            fns = []
            for j in range(DC):
                def f(e, j=j, p=p):
                    return e.transpose(ptrB[:, j * 128:(j + 1) * 128],
                                       xn16[p][:, j * 128:(j + 1) * 128], identb)
                fns.append(f)
            bb.group("tensor", fns, reads=(bxn[p], bid), writes=(bT,))
            v_op(lambda e, p=p: e.tensor_copy(
                xnfm[p].rearrange("p (a c) -> p a c", c=128),
                ptrB[:, 0:1024].rearrange("p (a c) -> p a c", c=128)),
                reads=(bT,), writes=(bxnfm[p],))

        def proj(p, wT, bwT, pd, bpd):
            fns = []
            for j in range(DC):
                for hh in range(2):
                    def f(e, j=j, hh=hh, wT=wT, pd=pd, p=p):
                        return e.matmul(pd[:, hh * 512:(hh + 1) * 512],
                                        xnfm[p][:, j * 128:(j + 1) * 128],
                                        wT[:, j * 1024 + hh * 512:
                                           j * 1024 + (hh + 1) * 512],
                                        start=(j == 0), stop=(j == DC - 1))
                    fns.append(f)
            bb.group("tensor", fns, reads=(bxnfm[p], bwT), writes=(bpd,))

        def rope(p, pd, bpd):
            r0 = rsc[p][:, 0:512]; r1 = rsc[p][:, 512:1024]
            rb = brsc[p]; ro = brot[p]; rt = rot16[p]
            ca, sa = tabc[p], tabs_[p]
            bca, bsa = btc[p], bts[p]
            i0, i1 = pd[:, 0:512], pd[:, 512:1024]
            o0, o1 = rt[:, 0:512], rt[:, 512:1024]
            v_op(lambda e: e.tensor_mul(r0, i0, ca), reads=(bpd, bca), writes=(rb,))
            v_op(lambda e: e.tensor_mul(r1, i1, sa), reads=(bpd, bsa), writes=(rb,))
            v_op(lambda e: e.tensor_sub(o0, r0, r1), reads=(rb,), writes=(ro,))
            v_op(lambda e: e.tensor_mul(r0, i0, sa), reads=(bpd, bsa), writes=(rb,))
            v_op(lambda e: e.tensor_mul(r1, i1, ca), reads=(bpd, bca), writes=(rb,))
            v_op(lambda e: e.tensor_add(o1, r0, r1), reads=(rb,), writes=(ro,))

        def headsplit(p, dsts, dbufs):
            for h in range(NH - 1):
                pb, pbb = (ptrB, bT) if h % 2 == 0 else (psmB, bM)
                t_op(lambda e, h=h, pb=pb, p=p: e.transpose(
                    pb[:, 0:128], rot16[p][:, h * 64:h * 64 + 128], identb),
                    reads=(brot[p], bid), writes=(pbb,))
                v_op(lambda e, h=h, pb=pb: e.tensor_copy(
                    dsts(h), pb[0:64, 0:128]),
                    reads=(pbb,), writes=(dbufs(h),))
                if h == NH - 2:
                    v_op(lambda e, pb=pb: e.tensor_copy(
                        dsts(NH - 1), pb[64:128, 0:128]),
                        reads=(pbb,), writes=(dbufs(NH - 1),))

        for tc in range(NT):
            p = tc % 2
            if tc == 0:
                ld("sync", tabc[0], t3["c1k"][0], btc[0])
                ld("sync", tabs_[0], t3["s1k"][0], bts[0])
            if tc + 1 < NT:
                ld("sync", tabc[(tc + 1) % 2], t3["c1k"][tc + 1], btc[(tc + 1) % 2])
                ld("sync", tabs_[(tc + 1) % 2], t3["s1k"][tc + 1], bts[(tc + 1) % 2])
            xa, xbuf = get_x(tc, "kv")
            ln_to_fm(p, tc, xa, xbuf)
            proj(p, wkT, bwkT, psb, bG)
            proj(p, wvT, bwvT, psA, bA)
            rope(p, psb, bG)
            v_op(lambda e, p=p: e.scalar_tensor_tensor(
                rsc[p][:, 0:1024], rot16[p], 1.0, rot16[p],
                op0=ALU.mult, op1=ALU.mult),
                reads=(brot[p],), writes=(brsc[p],))
            v_op(lambda e, p=p: e.tensor_reduce(
                kn_t, rsc[p][:, 0:1024].rearrange("p (h d) -> p h d", d=HD),
                axis=AX, op=ALU.add),
                reads=(brsc[p],), writes=(bknt,))
            if tc == 0:
                v_op(lambda e: e.tensor_copy(kn_run, kn_t), reads=(bknt,), writes=(bknr,))
            else:
                v_op(lambda e: e.tensor_max(kn_run, kn_run, kn_t),
                     reads=(bknt, bknr), writes=(bknr,))
            headsplit(p, lambda h, tc=tc: kaug[h][0:64, tc * 128:(tc + 1) * 128],
                      lambda h: kaug_b[h])
            v_op(lambda e, tc=tc: e.tensor_copy(
                vtm4[:, tc, :, 0:64],
                psA.rearrange("p (h c) -> p h c", c=64)),
                reads=(bA,), writes=(bvtm,))

        t_op(lambda e: e.transpose(psmF[0:16, 0:128], kn_run, identf),
             reads=(bknr, bidf), writes=(bM,))
        v_op(lambda e: e.reduce_max(v_mk[0:16, 0:1], psmF[0:16, 0:128], axis=AX),
             reads=(bM,), writes=(bmk,))
        t_op(lambda e: e.transpose(ptrF[0:1, 0:16], v_mk[0:16, 0:1], identf[0:16, 0:16]),
             reads=(bmk, bidf), writes=(bT,))
        v_op(lambda e: e.tensor_copy(v_mk[0:1, 0:16], ptrF[0:1, 0:16]),
             reads=(bT,), writes=(bmk,))
        t_op(lambda e: e.matmul(psmF[:, 0:16], ones_s, v_mk[0:1, 0:16],
                                start=True, stop=True),
             reads=(bmk, bone), writes=(bM,))
        v_op(lambda e: e.tensor_copy(mkb_s, psmF[:, 0:16]), reads=(bM,), writes=(bmkb,))
        bb.set_guard()

        if _DBG == "kv":
            toks = []
            for i, hh_ in enumerate((0, 15)):
                toks.append(bb.op("gpsimd", lambda e, i=i, hh_=hh_: e.dma_start(
                    out=out3[i][0:69, :], in_=kaug[hh_].bitcast(F32)),
                    reads=(kaug_b[hh_],), dma=True))
            vtf = vtm.bitcast(F32)
            for t in range(8):
                toks.append(bb.op("gpsimd", lambda e, t=t: e.dma_start(
                    out=out3[2 + t], in_=vtf[:, t * 1024:(t + 1) * 1024]),
                    reads=(bvtm,), dma=True))
            bb.op("gpsimd", lambda e: e.nop(), extra=tuple(toks))
            return "dbg"
        ar.release(m_kv)

        qaugs = [ar.b16(NH * 128, parts=69), ar.b16(NH * 128, parts=69)]
        bqaug = [Buf(), Buf()]
        beta16 = ar.b16(16)
        bbeta16 = Buf()
        for p_ in range(2):
            ld("sync", qaugs[p_][64:69, :], qaug_h[:], bqaug[p_])
        QOPS = int(_os.environ.get("KQOPS", 99))
        for qt in range(int(_os.environ.get("KQN", NT))):
            p = qt % 2
            sl = SL[p]
            if qt == 0:
                ld("sync", tabc[0], t3["c1q"][0], btc[0])
                ld("sync", tabs_[0], t3["s1q"][0], bts[0])
            if qt + 1 < NT:
                ld("sync", tabc[(qt + 1) % 2], t3["c1q"][qt + 1], btc[(qt + 1) % 2])
                ld("sync", tabs_[(qt + 1) % 2], t3["s1q"][qt + 1], bts[(qt + 1) % 2])
            xa, xbuf = get_x(qt, "q")
            if QOPS < 1:
                continue
            ln_to_fm(p, qt, xa, xbuf)
            if QOPS < 2:
                continue
            proj(p, wqT, bwqT, psb, bG)
            if QOPS < 3:
                continue
            rope(p, psb, bG)
            if QOPS < 4:
                continue
            v_op(lambda e, p=p: e.scalar_tensor_tensor(
                rsc[p][:, 0:1024], rot16[p], 1.0, rot16[p],
                op0=ALU.mult, op1=ALU.mult),
                reads=(brot[p],), writes=(brsc[p],))
            v_op(lambda e, p=p: e.tensor_reduce(
                qn_t, rsc[p][:, 0:1024].rearrange("p (h d) -> p h d", d=HD),
                axis=AX, op=ALU.add),
                reads=(brsc[p],), writes=(bqnt,))
            v_op(lambda e, sl=sl: e.tensor_mul(sl.beta, qn_t, mkb_s),
                 reads=(bqnt, bmkb), writes=(sl.bbeta,))
            s_op(lambda e, sl=sl: e.activation(out=sl.beta, in_=sl.beta,
                                               func=ACT.Ln, scale=1.0),
                 reads=(sl.bbeta,), writes=(sl.bbeta,))
            s_op(lambda e, sl=sl: e.activation(out=sl.beta, in_=sl.beta,
                                               func=ACT.Exp, scale=0.5),
                 reads=(sl.bbeta,), writes=(sl.bbeta,))
            v_op(lambda e, qt=qt, sl=sl: e.scalar_tensor_tensor(
                sl.beta, sl.beta, -1.0, qsl_s[:, qt * NH:(qt + 1) * NH],
                op0=ALU.mult, op1=ALU.add),
                reads=(sl.bbeta, bqsl), writes=(sl.bbeta,))
            if QOPS < 5:
                continue
            v_op(lambda e, sl=sl: e.tensor_copy(beta16[:, 0:16], sl.beta),
                 reads=(sl.bbeta,), writes=(bbeta16,))
            t_op(lambda e: e.transpose(ptrB[0:16, 0:128], beta16[:, 0:16], identb),
                 reads=(bbeta16, bid), writes=(bT,))
            v_op(lambda e, qt=qt: e.tensor_copy(
                betaT[0:16, qt * 128:(qt + 1) * 128], ptrB[0:16, 0:128]),
                reads=(bT,), writes=(bbetaT,))
            if QOPS < 6:
                continue
            headsplit(p, lambda h, p=p: qaugs[p][0:64, h * 128:(h + 1) * 128],
                      lambda h, p=p: bqaug[p])
            if QOPS < 7:
                continue
            bb.op("gpsimd", lambda e, qt=qt, p=p: e.dma_start(
                out=qsp[qt], in_=qaugs[p][0:69, :]),
                reads=(bqaug[p],), dma=True)
        bb.set_guard()
        if _DBG == "q":
            toks = []
            for i, hh_ in enumerate((0, 15)):
                toks.append(bb.op("gpsimd", lambda e, i=i, hh_=hh_: e.dma_start(
                    out=out3[i][0:69, :], in_=kaug[hh_].bitcast(F32)),
                    reads=(kaug_b[hh_],), extra=tuple(bb.guard), dma=True))
            bb.op("gpsimd", lambda e: e.nop(), extra=tuple(toks))
            return "dbg"

        ar.release(m_q0)

        qbuf = [ar.b16(S, parts=69), ar.b16(S, parts=69)]
        qb_b = [Buf(), Buf()]
        PTs = [ar.b16(S), ar.b16(S), ar.b16(S)]
        PT_b = [Buf(), Buf(), Buf()]
        avs = ar.b16(8 * S)
        bavs = Buf()
        rdv = [ar.f32(512)[0:1, :], ar.f32(512)[0:1, :]]
        brd = [Buf(), Buf()]
        bcs = [ar.f32(512), ar.f32(512)]
        bbcs = [Buf(), Buf()]
        avp = [(psA, 0), (psA, 512), (psB, 0), (psB, 512)]
        avp_b = [Buf(), Buf(), Buf(), Buf()]
        pSs = [(psb, 0), (psb, 512), (ptrF, 0), (psmF, 0)]
        pS_b = [Buf(), Buf(), bT, bM]
        gath = [[], []]
        last_rd = [None, None]
        csel = [0]

        def gather(h):
            p = h % 2
            toks = []
            g = (last_rd[p],) if last_rd[p] is not None else tuple(bb.guard)
            for tc in range(NT):
                toks.append(bb.op("sync", lambda e, tc=tc, h=h, p=p: e.dma_start(
                    out=qbuf[p][0:69, tc * 128:(tc + 1) * 128],
                    in_=qsp[tc][:, h * 128:(h + 1) * 128]),
                    extra=g, dma=True))
            gath[p] = toks
            bb.op("sync", lambda e, h=h, p=p: e.dma_start(
                out=qbuf[p][64:65, :], in_=betaT[h:h + 1, :]),
                reads=(bbetaT,), writes=(qb_b[p],), extra=g + tuple(toks),
                dma=True)

        last_sc = {}

        def emit_scores(h, kb):
            p = h % 2
            kbp = kb % 3
            q0 = kb * 128
            first = (kb == 0)
            off = 0
            while q0 + off < S:
                w = min(512, S - q0 - off)
                ps, po = pSs[csel[0] % 4]
                bps = pS_b[csel[0] % 4]
                csel[0] += 1
                ex = tuple(gath[p]) if first and off == 0 else ()
                last_sc[h] = t_op(
                    lambda e, h=h, kb=kb, q0=q0, off=off, w=w, ps=ps, po=po:
                    e.matmul(ps[0:128, po:po + w],
                             kaug[h][:, kb * 128:(kb + 1) * 128],
                             qbuf[h % 2][:, q0 + off:q0 + off + w],
                             start=True, stop=True),
                    reads=(kaug_b[h], qb_b[p]), writes=(bps,), extra=ex)
                if off == 0:
                    v_op(lambda e, ps=ps, po=po: e.tensor_add(
                        ps[:, po:po + 128], ps[:, po:po + 128], maskT_s),
                        reads=(bmskT,), writes=(bps,))
                s_op(lambda e, kbp=kbp, q0=q0, off=off, w=w, ps=ps, po=po:
                     e.activation(out=PTs[kbp][:, q0 + off:q0 + off + w],
                                  in_=ps[:, po:po + w], func=ACT.Exp, scale=1.0),
                     reads=(bps,), writes=(PT_b[kbp],))
                off += w

        def emit_av(h, kb):
            kbp = kb % 3
            for c in range(kb // 4, 4):
                lo = max(kb * 128, c * 512)
                hi = min(S, (c + 1) * 512)
                bank, boff = avp[c]
                t_op(lambda e, h=h, kb=kb, c=c, lo=lo, hi=hi, bank=bank, boff=boff,
                     kbp=kbp: e.matmul(
                         bank[0:65, boff + lo - c * 512:boff + hi - c * 512],
                         vtm[:, kb * 1040 + h * 65:kb * 1040 + (h + 1) * 65],
                         PTs[kbp][:, lo:hi],
                         start=(kb == 0), stop=(kb == min(15, 4 * c + 3))),
                     reads=(bvtm, PT_b[kbp]), writes=(avp_b[c],))

        def evac_chunk(h, c):
            rp = c % 2
            bank, boff = avp[c]
            pb, pbb = (ptrF, bT) if rp == 0 else (psmF, bM)
            v_op(lambda e, bank=bank, boff=boff, rp=rp: e.reciprocal(
                rdv[rp], bank[64:65, boff:boff + 512]),
                reads=(avp_b[c],), writes=(brd[rp],))
            t_op(lambda e, rp=rp, pb=pb: e.matmul(
                pb[0:64, 0:512], ones_s[0:1, 0:64], rdv[rp],
                start=True, stop=True),
                reads=(brd[rp], bone), writes=(pbb,))
            v_op(lambda e, rp=rp, pb=pb: e.tensor_copy(
                bcs[rp][0:64, :], pb[0:64, 0:512]),
                reads=(pbb,), writes=(bbcs[rp],))
            v_op(lambda e, h=h, c=c, bank=bank, boff=boff, rp=rp:
                 e.tensor_mul(
                     avs[(h % 2) * 64:(h % 2) * 64 + 64,
                         (h // 2) * S + c * 512:(h // 2) * S + (c + 1) * 512],
                     bank[0:64, boff:boff + 512], bcs[rp][0:64, :]),
                 reads=(avp_b[c], bbcs[rp]), writes=(bavs,))

        if _DBG == "pt":
            gather(0)
            for kb in range(15):
                emit_scores(0, kb)
            toks = [bb.op("gpsimd", lambda e: e.dma_start(
                out=out3[0], in_=PTs[0].bitcast(F32)),
                reads=(PT_b[0],), dma=True)]
            toks.append(bb.op("gpsimd", lambda e: e.dma_start(
                out=out3[1], in_=PTs[1].bitcast(F32)),
                reads=(PT_b[1],), dma=True))
            toks.append(bb.op("gpsimd", lambda e: e.dma_start(
                out=out3[2][0:69, :], in_=qbuf[0].bitcast(F32)),
                reads=(qb_b[0],), extra=tuple(gath[0]), dma=True))
            toks.append(bb.op("gpsimd", lambda e: e.dma_start(
                out=out3[3][0:69, :], in_=kaug[0].bitcast(F32)),
                reads=(kaug_b[0],), dma=True))
            bb.op("gpsimd", lambda e: e.nop(), extra=tuple(toks))
            return "dbg"

        PIPE = _os.environ.get("KNOPIPE", "") == ""
        gather(0)
        for h in range(NH):
            if h + 1 < NH:
                gather(h + 1)
            if PIPE:
                emit_scores(h, 0)
                emit_scores(h, 1)
                emit_scores(h, 2)
                for kb in range(NT):
                    emit_av(h, kb)
                    if kb + 3 < NT:
                        emit_scores(h, kb + 3)
                    if kb in (4, 8, 12):
                        evac_chunk(h, (kb - 4) // 4)
                evac_chunk(h, 3)
            else:
                for kb in range(NT):
                    emit_scores(h, kb)
                    emit_av(h, kb)
                for c in range(4):
                    evac_chunk(h, c)
            last_rd[h % 2] = last_sc[h]

        bb.set_guard()

        if _DBG == "qsp":
            toks = []
            for t in range(NT):
                toks.append(bb.op("gpsimd", lambda e, t=t: e.dma_start(
                    out=out3[t][0:69, :], in_=qsp[t].bitcast(F32)),
                    extra=tuple(bb.guard), dma=True))
            bb.op("gpsimd", lambda e: e.nop(), extra=tuple(toks))
            return "dbg"

        if _DBG == "avs":
            avf = avs.bitcast(F32)
            toks = []
            for t in range(8):
                toks.append(bb.op("gpsimd", lambda e, t=t: e.dma_start(
                    out=out3[t], in_=avf[:, t * 1024:(t + 1) * 1024]),
                    reads=(bavs,), dma=True))
            bb.op("gpsimd", lambda e: e.nop(), extra=tuple(toks))
            return "dbg"

        outb_s = ar.f32(1024)
        boutb = Buf()
        o2s = ar.f32(1024)
        bo2 = Buf()
        tt_sb = ar.f32(1024)
        btt = Buf()
        bD = [Buf(), Buf()]
        ld("sync", outb_s, outbr[:], boutb)
        for qt in range(NT):
            p = qt % 2
            sl = SL[p]
            xa, xbuf = get_x(qt, "e")
            psX = psA if p == 0 else psB
            fns = []
            for pr in range(8):
                for hh in range(2):
                    def f(e, pr=pr, hh=hh, qt=qt, psX=psX):
                        return e.matmul(psX[:, hh * 512:(hh + 1) * 512],
                                        avs[:, pr * S + qt * 128:pr * S + (qt + 1) * 128],
                                        woT[:, pr * 1024 + hh * 512:
                                            pr * 1024 + (hh + 1) * 512],
                                        start=(pr == 0), stop=(pr == 7))
                    fns.append(f)
            bb.group("tensor", fns, reads=(bavs, bwoT), writes=(bD[p],))
            v_op(lambda e, psX=psX: e.tensor_add(o2s, psX, outb_s),
                 reads=(bD[p], boutb), writes=(bo2,))
            stats(sl, o2s, bo2)
            rsqrt_chain(sl, sl.sg, sl.bsg, sl.mv[:, 1:2], sl.bmv, 1.0, 0)
            v_op(lambda e, sl=sl: e.tensor_scalar(sl.ngm, sl.mv[:, 0:1], sl.sg, -1.0,
                                                  op0=ALU.mult, op1=ALU.mult),
                 reads=(sl.bmv, sl.bsg), writes=(sl.bngm,))
            s_op(lambda e, sl=sl: e.activation(out=tt_sb, in_=o2s, func=ACT.Identity,
                                               scale=sl.sg, bias=sl.ngm),
                 reads=(bo2, sl.bsg, sl.bngm), writes=(btt,))
            tok = v_op(lambda e, xa=xa: e.tensor_add(xa, xa, tt_sb),
                       reads=(btt,), writes=(xbuf,))
            ln_precompute("f1", qt, xa, xbuf, sl,
                          var_scale=1.0 + 1e-3, eps_col=1)
            put_x(qt, tok, xa, xbuf)
        ar.release(m0)

    def get_x_l0(t, phase):
        return x_t[t], x_b[t]

    def put_x_l0(t, tok, xa, xbuf):
        pass

    for t in range(NT):
        ln_precompute("l0", t, x_t[t], x_b[t], SL[t % 2])
    attention(0, get_x_l0, put_x_l0, "l0", "f0")

    if _DBG == "l0a":
        toks = []
        for t in range(NT):
            toks.append(bb.op("gpsimd", lambda e, t=t: e.dma_start(out=out3[t], in_=x_t[t]),
                              reads=(x_b[t],), dma=True))
        bb.op("gpsimd", lambda e: e.nop(), extra=(toks[-1],))
        with nc.Block() as block:
            _realize(block)
        st.close()
        return nc

    for t in range(NT):
        bb.op("gpsimd", lambda e, t=t: e.dma_start(out=xsp3[t], in_=x_t[t]),
              reads=(x_b[t],), dma=True)
    bb.set_guard()
    ar.release(m_x)

    xw = [ar.f32(1024), ar.f32(1024)]
    xwb = [Buf(), Buf()]

    def xw_load(t, src3=xsp3):
        ld("sync", xw[t % 2], src3[t], xwb[t % 2])

    m_ffn0 = ar.mark()
    w1qT = ar.b16(DC * FF)
    bw1 = Buf()
    w2qT = ar.b16(32 * D)
    bw2 = Buf()
    b1_s = ar.f32(32)
    bb1 = Buf()
    b2_s = ar.f32(1024)
    bb2 = Buf()
    ld("sync", b1_s, b1l[:], bb1)
    ld("sync", b2_s, b2r[:], bb2)

    m_qscr = ar.mark()
    wstg = [ar.f32(2048), ar.f32(2048)]
    bwstg = [Buf(), Buf()]
    qa = ar.f32(2048)
    bqa = Buf()
    qb = ar.f32(2048)
    bqb = Buf()
    gp = ar.f32(48)
    bgp = Buf()
    wqn16 = ar.b16(2048)
    bwqn = Buf()
    W1r = W1h.rearrange("(a p) d -> p a d", p=128)
    W2r = W2h.rearrange("(a p) f -> a p f", p=128)

    for g in range(16):
        sl = wstg[g % 2]
        ld("gpsimd", sl.rearrange("p (a d) -> p a d", d=1024),
           W1r[:, 2 * g:2 * g + 2, :], bwstg[g % 2])
        s_op(lambda e, sl=sl, g=g: e.activation(out=qa, in_=sl, func=ACT.Abs,
                                                accum_out=gp[:, g:g + 1]),
             reads=(bwstg[g % 2],), writes=(bqa, bgp))
    for g in range(16):
        sl = wstg[g % 2]
        ld("gpsimd", sl, W2r[g // 2][:, (g % 2) * 2048:(g % 2 + 1) * 2048],
           bwstg[g % 2])
        s_op(lambda e, sl=sl, g=g: e.activation(out=qa, in_=sl, func=ACT.Abs,
                                                accum_out=gp[:, 16 + g:17 + g]),
             reads=(bwstg[g % 2],), writes=(bqa, bgp))
    for wi in range(2):
        cols = gp[:, 0:16] if wi == 0 else gp[:, 16:32]
        v_op(lambda e, cols=cols: e.reduce_sum(SL[0].tmp, cols, axis=AX),
             reads=(bgp,), writes=(SL[0].btmp,))
        t_op(lambda e: e.transpose(psmF[0:1, 0:128], SL[0].tmp, identf),
             reads=(SL[0].btmp, bidf), writes=(bM,))
        v_op(lambda e: e.reduce_sum(v_mk[0:1, 0:1], psmF[0:1, 0:128], axis=AX),
             reads=(bM,), writes=(bmk,))
        v_op(lambda e: e.tensor_scalar(v_mk[0:1, 1:2], v_mk[0:1, 0:1],
                                       1.0 / (FF * D), 1e-5, op0=ALU.mult, op1=ALU.add),
             reads=(bmk,), writes=(bmk,))
        v_op(lambda e: e.reciprocal(v_mk[0:1, 2:3], v_mk[0:1, 1:2]),
             reads=(bmk,), writes=(bmk,))
        t_op(lambda e: e.matmul(psmF[:, 0:1], ones_s, v_mk[0:1, 2:3],
                                start=True, stop=True),
             reads=(bmk, bone), writes=(bM,))
        v_op(lambda e, wi=wi: e.tensor_copy(gp[:, 40 + wi:41 + wi], psmF[:, 0:1]),
             reads=(bM,), writes=(bgp,))

    def quant(src, bsrc, n, rcol):
        v_op(lambda e, src=src, n=n, rcol=rcol: e.tensor_scalar(
            qa[:, 0:n], src, rcol, 0.5, op0=ALU.mult, op1=ALU.is_gt),
            reads=(bsrc, bgp), writes=(bqa,))
        v_op(lambda e, src=src, n=n, rcol=rcol: e.tensor_scalar(
            qb[:, 0:n], src, rcol, -0.5, op0=ALU.mult, op1=ALU.is_lt),
            reads=(bsrc, bgp), writes=(bqb,))
        v_op(lambda e, n=n: e.tensor_sub(wqn16[:, 0:n], qa[:, 0:n], qb[:, 0:n]),
             reads=(bqa, bqb), writes=(bwqn,))

    for c in range(32):
        sl = wstg[c % 2][:, 0:1024]
        ld("gpsimd", sl, W1r[:, c, :], bwstg[c % 2])
        quant(sl, bwstg[c % 2], 1024, gp[:, 40:41])
        pb, pbb = (ptrB, bT) if c % 2 == 0 else (psmB, bM)
        fns = []
        for j in range(DC):
            def f(e, j=j, pb=pb):
                return e.transpose(pb[:, j * 128:(j + 1) * 128],
                                   wqn16[:, j * 128:(j + 1) * 128], identb)
            fns.append(f)
        bb.group("tensor", fns, reads=(bwqn, bid), writes=(pbb,))
        dsel = w1qT.rearrange("p (a f) -> p a f", f=FF)[:, :, c * 128:(c + 1) * 128]
        v_op(lambda e, dsel=dsel, pb=pb: e.tensor_copy(
            dsel, pb[:, 0:1024].rearrange("p (a c) -> p a c", c=128)),
            reads=(pbb,), writes=(bw1,))
    for ch in range(16):
        dc, hf = ch // 2, ch % 2
        sl = wstg[ch % 2]
        ld("gpsimd", sl, W2r[dc][:, hf * 2048:(hf + 1) * 2048], bwstg[ch % 2])
        quant(sl, bwstg[ch % 2], 2048, gp[:, 41:42])
        for g in range(2):
            pb, pbb = (ptrB, bT) if g % 2 == 0 else (psmB, bM)
            fns = []
            for i in range(8):
                fc = hf * 16 + g * 8 + i
                def f(e, i2=g * 8 + i, i=i, pb=pb):
                    return e.transpose(pb[:, i * 128:(i + 1) * 128],
                                       wqn16[:, i2 * 128:(i2 + 1) * 128], identb)
                fns.append(f)
            bb.group("tensor", fns, reads=(bwqn, bid), writes=(pbb,))
            f0 = hf * 16 + g * 8
            dsel = w2qT[:, f0 * D:(f0 + 8) * D].rearrange(
                "p (a d) -> p a d", d=D)[:, :, dc * 128:(dc + 1) * 128]
            v_op(lambda e, dsel=dsel, pb=pb: e.tensor_copy(
                dsel, pb[:, 0:1024].rearrange("p (a c) -> p a c", c=128)),
                reads=(pbb,), writes=(bw2,))
    ar.release(m_qscr)
    bb.set_guard()

    xnfm_sp = ar.b16(DC * 512)
    bxnf = Buf()
    xe = [ar.f32(1024), ar.f32(1024)]
    xeb = [Buf(), Buf()]
    h_sp = ar.b16(32 * 512)
    bh = Buf()
    xn16f = ar.b16(D)
    bxn0 = Buf()
    o2f = ar.f32(1024)
    bo2f = Buf()
    ttf = ar.f32(1024)
    bttf = Buf()

    xw_load(0)
    for sp in range(4):
        for tl in range(4):
            tc = sp * 4 + tl
            if tc + 1 < NT:
                xw_load(tc + 1)
            xa, xbuf = xw[tc % 2], xwb[tc % 2]
            ln_apply_pre("f0", tc, xn16f, bxn0, xa, xbuf)
            pb, pbb = (ptrB, bT) if tl % 2 == 0 else (psmB, bM)
            fns = []
            for j in range(DC):
                def f(e, j=j, pb=pb):
                    return e.transpose(pb[:, j * 128:(j + 1) * 128],
                                       xn16f[:, j * 128:(j + 1) * 128], identb)
                fns.append(f)
            bb.group("tensor", fns, reads=(bxn0, bid), writes=(pbb,))
            dsel = xnfm_sp.rearrange("p (a t) -> p a t", t=512)[:, :, tl * 128:(tl + 1) * 128]
            v_op(lambda e, dsel=dsel, pb=pb: e.tensor_copy(
                dsel, pb[:, 0:1024].rearrange("p (a c) -> p a c", c=128)),
                reads=(pbb,), writes=(bxnf,))
        for fc in range(32):
            ps, bps = ((psA[:, 0:512], bA), (psA[:, 512:1024], bA),
                       (psB[:, 0:512], bB), (psB[:, 512:1024], bB))[fc % 4]
            fns = []
            for j in range(DC):
                def f(e, j=j, fc=fc, ps=ps):
                    return e.matmul(ps, w1qT[:, j * FF + fc * 128:j * FF + (fc + 1) * 128],
                                    xnfm_sp[:, j * 512:(j + 1) * 512],
                                    start=(j == 0), stop=(j == DC - 1))
                fns.append(f)
            bb.group("tensor", fns, reads=(bw1, bxnf), writes=(bps,))
            s_op(lambda e, fc=fc, ps=ps: e.activation(
                out=h_sp[:, fc * 512:(fc + 1) * 512], in_=ps, func=ACT.Relu,
                scale=1.0, bias=b1_s[:, fc:fc + 1]),
                reads=(bps, bb1), writes=(bh,))
        for tl in range(4):
            tc = sp * 4 + tl
            ld("sync", xe[tc % 2], xsp3[tc], xeb[tc % 2])
            fns = []
            for fc in range(32):
                for hh in range(2):
                    def f(e, fc=fc, hh=hh, tl=tl):
                        return e.matmul(
                            psb[:, hh * 512:(hh + 1) * 512],
                            h_sp[:, fc * 512 + tl * 128:fc * 512 + (tl + 1) * 128],
                            w2qT[:, fc * D + hh * 512:fc * D + (hh + 1) * 512],
                            start=(fc == 0), stop=(fc == 31))
                    fns.append(f)
            bb.group("tensor", fns, reads=(bh, bw2), writes=(bG,))
            v_op(lambda e: e.tensor_add(o2f, psb, b2_s), reads=(bG, bb2), writes=(bo2f,))
            stats(SL[tl % 2], o2f, bo2f)
            ln_apply(SL[tl % 2], ttf, bttf, o2f, bo2f)
            xa, xbuf = xe[tc % 2], xeb[tc % 2]
            v_op(lambda e, xa=xa: e.tensor_add(xa, xa, ttf),
                 reads=(bttf,), writes=(xbuf,))
            ln_precompute("l1", tc, xa, xbuf, SL[tl % 2])
            bb.op("gpsimd", lambda e, tc=tc, xa=xa: e.dma_start(out=xsp3[tc], in_=xa),
                  reads=(xbuf,), dma=True)
            if _DBG == "ffn0":
                bb.op("gpsimd", lambda e, tc=tc, xa=xa: e.dma_start(out=out3[tc], in_=xa),
                      reads=(xbuf,), dma=True)
    if _DBG == "ffn0":
        bb.op("gpsimd", lambda e: e.nop(), extra=(("dma_gpsimd", sch.ndma["gpsimd"]),))
        with nc.Block() as block:
            _realize(block)
        st.close()
        return nc
    bb.set_guard()
    ar.release(m_ffn0)

    _xl1 = {"kv": -1, "q": -1, "e": -1}

    def get_x_l1(t, phase):
        if _xl1[phase] < t:
            xw_load(t)
            _xl1[phase] = t
        if t + 1 < NT and _xl1[phase] < t + 1:
            xw_load(t + 1)
            _xl1[phase] = t + 1
        return xw[t % 2], xwb[t % 2]

    def put_x_l1(t, tok, xa, xbuf):
        bb.op("gpsimd", lambda e, t=t, xa=xa: e.dma_start(out=xsp3[t], in_=xa),
              reads=(xbuf,), dma=True)

    _r1 = attention1_st(get_x_l1, put_x_l1)
    if _r1 == "dbg":
        with nc.Block() as block:
            _realize(block)
        st.close()
        return nc

    if _DBG == "l1a":
        toks = []
        xw_load(0)
        for t in range(NT):
            if t + 1 < NT:
                xw_load(t + 1)
            toks.append(bb.op("gpsimd", lambda e, t=t: e.dma_start(
                out=out3[t], in_=xw[t % 2]), reads=(xwb[t % 2],), dma=True))
        bb.op("gpsimd", lambda e: e.nop(), extra=(toks[-1],))
        with nc.Block() as block:
            _realize(block)
        st.close()
        return nc
    bb.set_guard()

    m_f1 = ar.mark()
    xn_all = [ar.f32(1024) for _ in range(NT)]
    bxna = [Buf() for _ in range(NT)]
    wnat = ar.f32(DC * 1024)
    bwn = Buf()
    wq16 = ar.b16(DC * 1024)
    bwq = Buf()
    gam_s = ar.f32(1024)
    bgam = Buf()
    bet_s = ar.f32(1024)
    bbet = Buf()
    f1a = ar.f32(1024)
    bf1a = Buf()
    f1b = ar.f32(1024)
    bf1b = Buf()
    xqfm = ar.b16(DC * 128)
    bxqf = Buf()
    xq16 = ar.b16(D)
    bxq16 = Buf()
    mx_r = v_mk[:, 4:5]

    ld("sync", wnat.rearrange("p (a u) -> p a u", u=1024),
       wf1.rearrange("(a p) u -> p a u", p=128), bwn)
    ld("sync", gam_s, gamr[:], bgam)
    ld("sync", bet_s, betr[:], bbet)

    for j in range(DC):
        s_op(lambda e, j=j: e.activation(out=f1a, in_=wnat[:, j * 1024:(j + 1) * 1024],
                                         func=ACT.Abs, accum_out=kn_t[:, j:j + 1]),
             reads=(bwn,), writes=(bf1a, bknt))
    v_op(lambda e: e.reduce_sum(SL[0].tmp, kn_t[:, 0:8], axis=AX), reads=(bknt,), writes=(SL[0].btmp,))
    t_op(lambda e: e.transpose(psmF[0:1, 0:128], SL[0].tmp, identf), reads=(SL[0].btmp, bidf), writes=(bM,))
    v_op(lambda e: e.reduce_sum(v_mk[0:1, 0:1], psmF[0:1, 0:128], axis=AX),
         reads=(bM,), writes=(bmk,))
    v_op(lambda e: e.tensor_scalar(v_mk[0:1, 1:2], v_mk[0:1, 0:1], 1.0 / (D * D), 1e-5,
                                   op0=ALU.mult, op1=ALU.add),
         reads=(bmk,), writes=(bmk,))
    v_op(lambda e: e.reciprocal(v_mk[0:1, 2:3], v_mk[0:1, 1:2]), reads=(bmk,), writes=(bmk,))
    t_op(lambda e: e.matmul(psmF[:, 0:1], ones_s, v_mk[0:1, 2:3], start=True, stop=True),
         reads=(bmk, bone), writes=(bM,))
    v_op(lambda e: e.tensor_copy(pv_s[:, 2:3], psmF[:, 0:1]), reads=(bM,), writes=(bpv,))
    for j in range(DC):
        wsl = wnat[:, j * 1024:(j + 1) * 1024]
        v_op(lambda e, wsl=wsl: e.tensor_scalar(f1a, wsl, pv_s[:, 2:3], 0.5,
                                                op0=ALU.mult, op1=ALU.is_gt),
             reads=(bwn, bpv), writes=(bf1a,))
        v_op(lambda e, wsl=wsl: e.tensor_scalar(f1b, wsl, pv_s[:, 2:3], -0.5,
                                                op0=ALU.mult, op1=ALU.is_lt),
             reads=(bwn, bpv), writes=(bf1b,))
        v_op(lambda e, j=j: e.tensor_sub(wq16[:, j * 1024:(j + 1) * 1024], f1a, f1b),
             reads=(bf1a, bf1b), writes=(bwq,))

    xw_load(0)
    for tc in range(NT):
        if tc + 1 < NT:
            xw_load(tc + 1)
        xa, xbuf = xw[tc % 2], xwb[tc % 2]
        ln_apply_pre("f1", tc, f1a, bf1a, xa, xbuf)
        v_op(lambda e, tc=tc: e.tensor_mul(xn_all[tc], f1a, gam_s),
             reads=(bf1a, bgam), writes=(bxna[tc],))
        v_op(lambda e, tc=tc: e.tensor_add(xn_all[tc], xn_all[tc], bet_s),
             reads=(bbet,), writes=(bxna[tc],), extra=tuple(bxna[tc].w))
        v_op(lambda e, tc=tc: e.tensor_reduce(kn_t[:, 15:16], xn_all[tc], axis=AX,
                                              op=ALU.max, apply_absolute_value=True),
             reads=(bxna[tc],), writes=(bknt,))
        if tc == 0:
            v_op(lambda e: e.tensor_copy(mx_r, kn_t[:, 15:16]), reads=(bknt,), writes=(bmk,))
        else:
            v_op(lambda e: e.tensor_max(mx_r, mx_r, kn_t[:, 15:16]),
                 reads=(bknt,), writes=(bmk,), extra=tuple(bmk.w))
    t_op(lambda e: e.transpose(psmF[0:1, 0:128], mx_r, identf), reads=(bmk, bidf), writes=(bM,))
    v_op(lambda e: e.reduce_max(v_mk[0:1, 6:7], psmF[0:1, 0:128], axis=AX),
         reads=(bM,), writes=(bmk,))
    bcc = Buf()
    bb.op("gpsimd", lambda e: e.dma_start(out=ccin[:], in_=v_mk[0:1, 6:7]),
          reads=(bmk,), dma=True)
    bb.op("gpsimd", lambda e: e.collective_compute(
        "AllReduce", ALU.max, replica_groups=[[0, 1, 2, 3, 4, 5, 6, 7]],
        ins=[ccin[:]], outs=[ccout[:]]),
        reads=(), writes=(bcc,), extra=tuple(sch.dma_all_tokens("gpsimd")))
    bb.op("gpsimd", lambda e: e.dma_start(out=v_mk[0:1, 7:8], in_=ccout[:]),
          reads=(bcc,), writes=(bmk,), dma=True)
    v_op(lambda e: e.reciprocal(v_mk[0:1, 8:9], v_mk[0:1, 7:8]), reads=(bmk,), writes=(bmk,))
    v_op(lambda e: e.tensor_scalar(v_mk[0:1, 9:10], v_mk[0:1, 8:9], 128.0, None,
                                   op0=ALU.mult),
         reads=(bmk,), writes=(bmk,))
    v_op(lambda e: e.tensor_scalar(v_mk[0:1, 10:11], v_mk[0:1, 1:2], 1.0, -1e-5,
                                   op0=ALU.mult, op1=ALU.add),
         reads=(bmk,), writes=(bmk,))
    v_op(lambda e: e.tensor_mul(v_mk[0:1, 11:12], v_mk[0:1, 10:11], v_mk[0:1, 7:8]),
         reads=(bmk,), writes=(bmk,), extra=tuple(bmk.w))
    v_op(lambda e: e.tensor_scalar(v_mk[0:1, 12:13], v_mk[0:1, 11:12], 1.0 / 128.0,
                                   None, op0=ALU.mult),
         reads=(bmk,), writes=(bmk,), extra=tuple(bmk.w))
    t_op(lambda e: e.matmul(psmF[:, 0:2], ones_s, v_mk[0:1, 9:13:3], start=True, stop=True),
         reads=(bmk, bone), writes=(bM,))
    sv_c = pv_s[:, 0:1]
    osc_c = pv_s[:, 1:2]
    v_op(lambda e: e.tensor_copy(sv_c, psmF[:, 0:1]), reads=(bM,), writes=(bpv,))
    v_op(lambda e: e.tensor_copy(osc_c, psmF[:, 1:2]), reads=(bM,), writes=(bpv,),
         extra=tuple(bpv.w))

    CLIP = 128.0 - 1e-5
    xw_load(0)
    ost = []
    for tc in range(NT):
        if tc + 1 < NT:
            xw_load(tc + 1)
        v_op(lambda e, tc=tc: e.tensor_scalar(f1a, xn_all[tc], sv_c, CLIP,
                                              op0=ALU.mult, op1=ALU.min),
             reads=(bxna[tc], bpv), writes=(bf1a,))
        v_op(lambda e: e.tensor_scalar(xq16, f1a, -CLIP, None, op0=ALU.max),
             reads=(bf1a,), writes=(bxq16,))
        pb, pbb = (ptrB, bT) if tc % 2 == 0 else (psmB, bM)
        fns = []
        for j in range(DC):
            def f(e, j=j, pb=pb):
                return e.transpose(pb[:, j * 128:(j + 1) * 128],
                                   xq16[:, j * 128:(j + 1) * 128], identb)
            fns.append(f)
        bb.group("tensor", fns, reads=(bxq16, bid), writes=(pbb,))
        v_op(lambda e, pb=pb: e.tensor_copy(
            xqfm.rearrange("p (a c) -> p a c", c=128),
            pb[:, 0:1024].rearrange("p (a c) -> p a c", c=128)),
            reads=(pbb,), writes=(bxqf,))
        fns = []
        for j in range(DC):
            for hh in range(2):
                def f(e, j=j, hh=hh):
                    return e.matmul(psb[:, hh * 512:(hh + 1) * 512],
                                    xqfm[:, j * 128:(j + 1) * 128],
                                    wq16[:, j * 1024 + hh * 512:j * 1024 + (hh + 1) * 512],
                                    start=(j == 0), stop=(j == DC - 1))
                fns.append(f)
        bb.group("tensor", fns, reads=(bxqf, bwq), writes=(bG,))
        sl = SL[tc % 2]
        stats(sl, psb, bG)
        v_op(lambda e, sl=sl: e.tensor_mul(sl.l2, osc_c, osc_c), reads=(bpv,), writes=(sl.bl2,))
        v_op(lambda e, sl=sl: e.tensor_mul(sl.tmp, sl.mv[:, 1:2], sl.l2),
             reads=(sl.bmv, sl.bl2), writes=(sl.btmp,))
        s_op(lambda e, sl=sl: e.activation(out=sl.l2, in_=sl.tmp, func=ACT.Ln, scale=1.0,
                                           bias=epsv_s[:, 0:1]),
             reads=(sl.btmp, beps), writes=(sl.bl2,))
        s_op(lambda e, sl=sl: e.activation(out=sl.sg, in_=sl.l2, func=ACT.Exp, scale=-0.5),
             reads=(sl.bl2,), writes=(sl.bsg,))
        v_op(lambda e, sl=sl: e.tensor_mul(sl.sg, sl.sg, osc_c), reads=(bpv,), writes=(sl.bsg,),
             extra=tuple(sl.bsg.w))
        v_op(lambda e, sl=sl: e.tensor_scalar(sl.ngm, sl.mv[:, 0:1], sl.sg, -1.0,
                                              op0=ALU.mult, op1=ALU.mult),
             reads=(sl.bmv, sl.bsg), writes=(sl.bngm,))
        s_op(lambda e, sl=sl: e.activation(out=f1a, in_=psb, func=ACT.Identity,
                                           scale=sl.sg, bias=sl.ngm),
             reads=(bG, sl.bsg, sl.bngm), writes=(bf1a,))
        xa, xbuf = xw[tc % 2], xwb[tc % 2]
        v_op(lambda e, xa=xa: e.tensor_add(xa, xa, f1a), reads=(bf1a,), writes=(xbuf,))
        tok = bb.op("gpsimd", lambda e, tc=tc, xa=xa: e.dma_start(out=out3[tc], in_=xa),
                    reads=(xbuf,), dma=True)
        ost.append(tok)

    bb.op("gpsimd", lambda e: e.nop(), extra=tuple(ost))

    with nc.Block() as block:
        _realize(block)
    st.close()
    return nc


_PROG = None


def kernel(L0_attn_x, L0_attn_w_q, L0_attn_w_k, L0_attn_w_v,
           L0_ffn_W1, L0_ffn_b1, L0_ffn_W2, L0_ffn_b2,
           L1_attn_wq, L1_attn_wk, L1_attn_wv, L1_attn_out_w, L1_attn_out_b,
           L1_ffn_weight, L1_ffn_ln_gamma, L1_ffn_ln_beta):
    global _PROG
    c = host_consts()
    if _PROG is None or _PROG[0] != _DBG:
        _PROG = (_DBG, build_program())
    nc = _PROG[1]

    f32 = np.float32
    x = np.ascontiguousarray(np.asarray(L0_attn_x, f32))
    common = {
        "w0q": np.ascontiguousarray(np.asarray(L0_attn_w_q, f32)),
        "w0k": np.ascontiguousarray(np.asarray(L0_attn_w_k, f32)),
        "w0v": np.ascontiguousarray(np.asarray(L0_attn_w_v, f32)),
        "W1": np.ascontiguousarray(np.asarray(L0_ffn_W1, f32)),
        "b1l": np.ascontiguousarray(np.asarray(L0_ffn_b1, f32).reshape(32, 128).T),
        "W2": np.ascontiguousarray(np.asarray(L0_ffn_W2, f32)),
        "b2r": np.ascontiguousarray(np.broadcast_to(np.asarray(L0_ffn_b2, f32), (128, D))),
        "w1q": np.ascontiguousarray(np.asarray(L1_attn_wq, f32)),
        "w1k": np.ascontiguousarray(np.asarray(L1_attn_wk, f32)),
        "w1v": np.ascontiguousarray(np.asarray(L1_attn_wv, f32)),
        "w1o": np.ascontiguousarray(np.asarray(L1_attn_out_w, f32)),
        "outbr": np.ascontiguousarray(np.broadcast_to(np.asarray(L1_attn_out_b, f32), (128, D))),
        "wf1": np.ascontiguousarray(np.asarray(L1_ffn_weight, f32)),
        "gamr": np.ascontiguousarray(np.broadcast_to(np.asarray(L1_ffn_ln_gamma, f32), (128, D))),
        "betr": np.ascontiguousarray(np.broadcast_to(np.asarray(L1_ffn_ln_beta, f32), (128, D))),
        "identb": c["identb"], "identf": c["identf"], "maskt": c["mask"],
        "masktt": c["maskT"], "onesb": c["onesb"],
        "kaugt": c["kaug"], "qaugt": c["qaug"], "qslt": c["qslope"],
        "onesf": c["onesf"],
        "epsv": np.ascontiguousarray(
            np.broadcast_to(np.array([1e-5, 1e-8], np.float32), (128, 2))),
    }
    for k in ("c0k", "s0k", "c0q", "s0q", "c1k", "s1k", "c1q", "s1q"):
        common["t_" + k] = c[k]

    in_maps = []
    for ci in range(8):
        m = dict(common)
        m["x_in"] = np.ascontiguousarray(x[ci])
        in_maps.append(m)

    res = run_bass_kernel_spmd(nc, in_maps, core_ids=list(range(8)))
    outs = [np.asarray(res.results[ci]["out"], f32) for ci in range(8)]
    return np.stack(outs, axis=0)
